# version 13
# baseline (speedup 1.0000x reference)
import numpy as np
import ml_dtypes
import concourse.bass as bass
import concourse.mybir as mybir
from concourse.bass_utils import run_bass_kernel_spmd

F32 = mybir.dt.float32
BF16 = mybir.dt.bfloat16
AF = mybir.ActivationFunctionType
ALU = mybir.AluOpType
AX = mybir.AxisListType


class CFG:
    S = 4096
    D = 1024
    CH = 8
    H = 16
    DH = 64
    DEPTH = 6
    WIN = 256
    TB = 1024
    QB = 1024
    DI = 2730
    F1H = 2816
    NCA = 22
    NC2 = 22
    SAMPLE = 16
    LN_EPS = 1e-5


ENG2BLOCK = {"pe": "tensor", "act": "scalar", "dve": "vector", "pool": "gpsimd", "sync": "sync"}
Q2ENG = {"qL": "sync", "qS": "act", "qG": "pool", "qW": "sync", "qX": "pool", "qV": "act", "qF": "sync"}
STREAMS = ("pe", "act", "dve", "pool", "sync")
ALLSEMS = ("pe", "act", "dve", "pool", "qL", "qS", "qG")


class Tok:
    __slots__ = ("sem", "val")
    def __init__(self, sem, val):
        self.sem = sem
        self.val = val
    def __repr__(self):
        return f"{self.sem}>={self.val}"


class Buf:
    __slots__ = ("ready", "readers", "name")
    def __init__(self, name=""):
        self.ready = {}
        self.readers = []
        self.name = name
    def rdeps(self):
        return [Tok(s, v) for s, v in self.ready.items()]

    @property
    def ready1(self):
        if not self.ready:
            return None
        assert len(self.ready) == 1, f"multi-ready buf {self.name}"
        ((s, v),) = self.ready.items()
        return Tok(s, v)
    def wdeps(self):
        return list(self.readers) + self.rdeps()
    def wrote(self, tok):
        if tok is not None:
            self.ready = {tok.sem: tok.val}
            self.readers = []
    def wrote_add(self, tok):
        if tok is not None:
            if self.ready.get(tok.sem, -1) < tok.val:
                self.ready[tok.sem] = tok.val
    def read(self, tok):
        if tok is not None:
            self.readers.append(tok)
            if len(self.readers) > 24:
                self.readers = self.readers[-24:]


class Prog:
    def __init__(self, nc):
        self.nc = nc
        self.ops = {e: [] for e in STREAMS}
        self.cnt = {s: 0 for s in ("pe", "act", "dve", "pool")}
        self.hwm = {e: {} for e in STREAMS}
        self.sems = {}
        self.nops = 0

    def make_sems(self, es):
        self._es = es
        for k in ("pe", "act", "dve", "pool"):
            self.sems[k] = es.enter_context(self.nc.semaphore("sem_" + k))

    def _dma_sem(self, key):
        if key not in self.sems:
            self.sems[key] = self._es.enter_context(self.nc.semaphore("sem_" + key))
            self.cnt[key] = 0
        return self.sems[key]

    def _prep(self, stream, deps, allow_embed=True):
        hw = self.hwm[stream]
        best = {}
        for t in deps:
            if t is None:
                continue
            if hw.get(t.sem, -1) >= t.val:
                continue
            if best.get(t.sem, -1) < t.val:
                best[t.sem] = t.val
        todo = [Tok(s, v) for s, v in best.items()]
        for t in todo:
            hw[t.sem] = t.val
        emb = todo.pop() if (todo and allow_embed) else None
        return todo, emb

    def op(self, stream, fn, deps=(), inc=True):
        pre, emb = self._prep(stream, deps)
        tok = None
        if inc:
            self.cnt[stream] += 1
            tok = Tok(stream, self.cnt[stream])
        sems = self.sems
        def closure(eng, fn=fn, pre=pre, emb=emb, tok=tok, stream=stream):
            for t in pre:
                eng.wait_ge(sems[t.sem], t.val)
            ins = fn(eng)
            if emb is not None:
                ins._wait_ge(sems[emb.sem], emb.val)
            if tok is not None:
                ins.then_inc(sems[stream], 1)
            return ins
        self.ops[stream].append(closure)
        self.nops += 1
        return tok

    def tok(self, stream):
        return Tok(stream, self.cnt[stream])

    def wait(self, stream, deps):
        pre, _ = self._prep(stream, deps, allow_embed=False)
        if not pre:
            return
        sems = self.sems
        def closure(eng, pre=pre):
            for t in pre:
                eng.wait_ge(sems[t.sem], t.val)
        self.ops[stream].append(closure)

    def dma(self, q, out, in_, deps=(), **kw):
        import re as _re
        stream = Q2ENG[q]
        side = out.tensor
        if type(side).__name__.startswith("DRam"):
            side = in_.tensor
        key = "d_" + q + "_" + _re.sub(r"_\d+$", "", side.name)
        self._dma_sem(key)
        pre, _ = self._prep(stream, deps, allow_embed=False)
        self.cnt[key] += 16
        tok = Tok(key, self.cnt[key])
        sems = self.sems
        def closure(eng, out=out, in_=in_, pre=pre, key=key, kw=kw):
            for t in pre:
                eng.wait_ge(sems[t.sem], t.val)
            ins = eng.dma_start(out=out, in_=in_, **kw)
            ins.then_inc(sems[key], 16)
            return ins
        self.ops[stream].append(closure)
        self.nops += 1
        return tok

    def barrier(self):
        toks = [Tok(s, v) for s, v in self.cnt.items() if v > 0]
        for e in STREAMS:
            self.wait(e, toks)

    def finalize_block(self, es):
        block = es.enter_context(self.nc.Block())
        for e, attr in ENG2BLOCK.items():
            ops = self.ops[e]
            def run(eng, ops=ops):
                for f in ops:
                    f(eng)
            getattr(block, attr)(run)


def host_constants(S):
    D, DH, WIN = CFG.D, CFG.DH, CFG.WIN
    c = {}
    c["ident32"] = np.eye(128, dtype=np.float32)
    c["identbf"] = np.eye(128, dtype=ml_dtypes.bfloat16)
    c["ones_1x128"] = np.ones((1, 128), dtype=ml_dtypes.bfloat16)
    c["ones_128x1"] = np.ones((128, 1), dtype=ml_dtypes.bfloat16)
    half = np.arange(0, DH, 2, dtype=np.float64)
    inv_freq = 1.0 / 10000.0 ** (half / DH)
    t = np.arange(S, dtype=np.float64)
    freqs = t[:, None] * inv_freq[None, :]
    freqs = np.concatenate([freqs, freqs], -1)
    base_scale = (half + 0.4 * DH) / (1.4 * DH)
    power = (t - S // 2) / (WIN // 2)
    scale = base_scale[None, :] ** power[:, None]
    scale = np.concatenate([scale, scale], -1)
    cos, sin = np.cos(freqs), np.sin(freqs)
    def fm(a):
        return np.ascontiguousarray(np.tile(a.T, (2, 1))).astype(ml_dtypes.bfloat16)
    c["tab_cs"] = fm(cos * scale)
    c["tab_ss"] = fm(sin * scale)
    c["tab_ci"] = fm(cos / scale)
    c["tab_si"] = fm(sin / scale)
    R = np.zeros((128, 128), dtype=np.float32)
    for base in (0, 64):
        for f in range(32):
            R[base + f + 32, base + f] = -1.0
        for f in range(32, 64):
            R[base + f - 32, base + f] = 1.0
    c["rperm"] = R.astype(ml_dtypes.bfloat16)
    esel = np.zeros((CFG.CH, 128, 16), dtype=np.float32)
    for cc in range(CFG.CH):
        esel[cc, 0:64, 2 * cc] = 1.0
        esel[cc, 64:128, 2 * cc + 1] = 1.0
    c["esel"] = esel.astype(ml_dtypes.bfloat16)
    epr = np.zeros((CFG.CH, 16, 128), dtype=np.float32)
    for cc in range(CFG.CH):
        epr[cc, 2 * cc, 0:64] = 1.0
        epr[cc, 2 * cc + 1, 64:128] = 1.0
    c["eprime"] = epr.astype(ml_dtypes.bfloat16)
    m = np.zeros((128, 1024), dtype=np.float32)
    for kt in range(4):
        for pp in range(128):
            qw = np.arange(256)
            delta = qw + 256 - kt * 128 - pp
            ok = (delta >= 0) & (delta <= 256)
            m[pp, kt * 256:(kt + 1) * 256] = ok.astype(np.float32)
    c["maskT"] = m.astype(ml_dtypes.bfloat16)
    m0 = m.copy()
    m0[:, 0:512] = 0.0
    c["maskW0"] = m0.astype(ml_dtypes.bfloat16)
    return c


class Kern:
    def __init__(self, nc, es, p):
        self.nc = nc
        self.es = es
        self.p = p
        self.bufs = {}
        self.S = CFG.S
        self._scopes = []

    def sb(self, name, shape, dtype):
        es = self._scopes[-1] if self._scopes else self.es
        t = es.enter_context(self.nc.sbuf_tensor(name, shape, dtype))
        self.bufs[name] = Buf(name)
        return t

    def psum(self, name, shape, dtype=F32):
        t = self.es.enter_context(self.nc.psum_tensor(name, shape, dtype))
        self.bufs[name] = Buf(name)
        self.bufs[name + ":0"] = Buf(name + ":0")
        self.bufs[name + ":1"] = Buf(name + ":1")
        return t

    def pb(self, ps, r):
        return self.bufs[ps.name + f":{r}"]

    def pwd(self, ps, regs=(0, 1)):
        d = []
        for r in regs:
            d += self.pb(ps, r).wdeps()
        return d

    def prd(self, ps, regs=(0, 1)):
        d = []
        for r in regs:
            d += self.pb(ps, r).rdeps()
        return d

    def pwrote(self, ps, tok, regs=(0, 1)):
        for r in regs:
            self.pb(ps, r).wrote(tok)

    def pwrote0(self, ps, tok):
        self.pwrote(ps, tok, (0,))

    def pread0(self, ps, tok):
        self.pread(ps, tok, (0,))

    def pread(self, ps, tok, regs=(0, 1)):
        for r in regs:
            self.pb(ps, r).read(tok)

    def dram(self, name, shape, dtype, kind="Internal"):
        t = self.nc.dram_tensor(name, shape, dtype, kind=kind)
        self.bufs[name] = Buf(name)
        return t

    def B(self, x):
        return self.bufs[x if isinstance(x, str) else x.name]

    from contextlib import contextmanager
    @contextmanager
    def scope(self):
        from contextlib import ExitStack
        es = ExitStack()
        self._scopes.append(es)
        try:
            with es:
                yield
        finally:
            self._scopes.pop()
            self.p.barrier()

    def build(self):
        p, S = self.p, self.S
        self.x_in = self.dram("x_in", [S, CFG.D], F32, kind="ExternalInput")
        self.wq_in = self.dram("wq_in", [CFG.D, CFG.D], F32, kind="ExternalInput")
        self.wk_in = self.dram("wk_in", [CFG.D, CFG.D], F32, kind="ExternalInput")
        self.wv_in = self.dram("wv_in", [CFG.D, CFG.D], F32, kind="ExternalInput")
        self.woo_in = self.dram("woo_in", [CFG.D, CFG.D], F32, kind="ExternalInput")
        self.gamma_in = self.dram("gamma_bc", [128, CFG.D], F32, kind="ExternalInput")
        self.beta_in = self.dram("beta_bc", [128, CFG.D], F32, kind="ExternalInput")
        self.wqkv_in = self.dram("wqkv_in", [CFG.DEPTH, CFG.D, 3 * CFG.D], F32, kind="ExternalInput")
        self.qs_in = self.dram("qs_in", [CFG.DEPTH, CFG.DH], F32, kind="ExternalInput")
        self.ks_in = self.dram("ks_in", [CFG.DEPTH, CFG.DH], F32, kind="ExternalInput")
        self.wo_in = self.dram("wo_in", [CFG.DEPTH, CFG.D, CFG.D], F32, kind="ExternalInput")
        self.lng_in = self.dram("lng_in", [CFG.DEPTH, CFG.D], F32, kind="ExternalInput")
        self.lnb_in = self.dram("lnb_in", [CFG.DEPTH, CFG.D], F32, kind="ExternalInput")
        self.w1_in = self.dram("w1_in", [CFG.DEPTH, CFG.D, 2 * CFG.DI], F32, kind="ExternalInput")
        self.b1p_in = self.dram("b1p_in", [CFG.DEPTH, 2 * CFG.F1H], F32, kind="ExternalInput")
        self.w2_in = self.dram("w2_in", [CFG.DEPTH, CFG.DI, CFG.D], F32, kind="ExternalInput")
        self.b2_in = self.dram("b2_in", [CFG.DEPTH, CFG.D], F32, kind="ExternalInput")
        for cname, shape, dt in [
            ("ident32", [128, 128], F32), ("identbf", [128, 128], BF16),
            ("ones_1x128", [1, 128], BF16), ("ones_128x1", [128, 1], BF16),
            ("tab_cs", [128, S], BF16), ("tab_ss", [128, S], BF16),
            ("tab_ci", [128, S], BF16), ("tab_si", [128, S], BF16),
            ("rperm", [128, 128], BF16), ("esel", [CFG.CH, 128, 16], BF16),
            ("eprime", [CFG.CH, 16, 128], BF16),
            ("maskT", [128, 1024], BF16), ("maskW0", [128, 1024], BF16),
        ]:
            setattr(self, "d_" + cname, self.dram(cname, shape, dt, kind="ExternalInput"))
        self.out_t = self.dram("out_t", [S, CFG.D], F32, kind="ExternalOutput")

        dr = self.dram
        CHT = S // 128
        self.xn_tm = dr("xn_tm", [CHT, 128, CFG.D], F32)
        self.xn_bf = dr("xn_bf", [CFG.CH, S // 512, 128, 512], BF16)
        self.y0 = dr("y0", [CFG.CH, 128, S], F32)
        self.yA = dr("yA", [CFG.CH, 128, S], F32)
        self.yB = dr("yB", [CFG.CH, 128, S], F32)
        self.yM = dr("yM", [CFG.CH, 128, S], F32)
        self.q_d = dr("q_d", [CFG.H, S // 512, 64, 512], BF16)
        self.k_d = dr("k_d", [CFG.H, S // 512, 64, 512], BF16)
        self.v_d = dr("v_d", [CHT, 128, 1040], BF16)
        self.o2_d = dr("o2_d", [CFG.CH, S // 512, 128, 512], BF16)
        self.s_tm = dr("s_tm", [CHT, 128, CFG.D], F32)
        self.wq_bf = dr("wq_bf", [CFG.CH, 2, 128, 512], BF16)
        self.wk_bf = dr("wk_bf", [CFG.CH, 2, 128, 512], BF16)
        self.wv_bf = dr("wv_bf", [CFG.CH, 2, 128, 512], BF16)
        self.woo_bf = dr("woo_bf", [CFG.CH, 2, 128, 512], BF16)
        self.wqkv_bf = dr("wqkv_bf", [CFG.DEPTH, CFG.CH, 6, 128, 512], BF16)
        self.wo_bf = dr("wo_bf", [CFG.DEPTH, CFG.CH, 2, 128, 512], BF16)
        self.w1_bf = dr("w1_bf", [CFG.DEPTH, CFG.CH, 11, 128, 512], BF16)
        self.w2_bf = dr("w2_bf", [CFG.DEPTH, CFG.NC2, 2, 128, 512], BF16)

        sb = self.sb
        self.c_id32 = sb("c_id32", [128, 128], F32)
        self.c_idbf = sb("c_idbf", [128, 128], BF16)
        self.c_ones1x = sb("c_ones1x", [1, 128], BF16)
        self.c_ones128 = sb("c_ones128", [128, 1], BF16)
        self.c_ones65 = sb("c_ones65", [65, 128], BF16)
        self.c_rperm = sb("c_rperm", [128, 128], BF16)
        self.c_esel = sb("c_esel", [128, CFG.CH * 16], BF16)
        self.c_eprime = sb("c_eprime", [16, CFG.CH * 128], BF16)
        self.c_maskT = sb("c_maskT", [128, 1024], BF16)
        self.c_maskW0 = sb("c_maskW0", [128, 1024], BF16)
        self.lng_sb = sb("lng_sb", [128, CFG.DEPTH * CFG.CH], F32)
        self.negc_sb = sb("negc_sb", [128, CFG.DEPTH], F32)
        self.qs8_sb = sb("qs8_sb", [128, CFG.DEPTH], F32)
        self.qs8p_sb = sb("qs8p_sb", [128, CFG.DEPTH], F32)
        self.ks_sb = sb("ks_sb", [128, CFG.DEPTH], F32)
        self.ksp_sb = sb("ksp_sb", [128, CFG.DEPTH], F32)
        self.b2_sb = sb("b2_sb", [128, CFG.DEPTH * CFG.CH], F32)
        self.b1p_sb = sb("b1p_sb", [128, CFG.DEPTH * 44], F32)
        self.wch = [sb("wch0", [128, 1536], F32), sb("wch1", [128, 1536], F32)]
        self.wch_par = 0
        self.w16 = sb("w16", [128, 3072], BF16)
        self.ia_xi = sb("ia_xi", [128, 8 * CFG.TB], BF16)
        self.ia_yt = [sb("ia_yt0", [128, CFG.TB + 1], F32), sb("ia_yt1", [128, CFG.TB + 1], F32)]
        self.ia_ym = sb("ia_ym", [128, 512], F32)
        self.ia_yr = [sb("ia_yr0", [128, 512], F32), sb("ia_yr1", [128, 512], F32)]
        self.fs_rows = sb("fs_rows", [65, S], BF16)
        self.fs_tmp = sb("fs_tmp", [1, 1024], F32)
        self.fs_x2 = sb("fs_x2", [128, 512], BF16)

        self.ps_a = self.psum("ps_a", [128, 1024], F32)
        self.ps_b = self.psum("ps_b", [128, 1024], F32)
        self.ps_c = self.psum("ps_c", [128, 1024], F32)
        self.ps_m = self.psum("ps_m", [128, 512], F32)
        self.ps_t = self.psum("ps_t", [128, 1024], BF16)

        import os
        phlim = int(os.environ.get("KPHASES", "999"))
        phn = [0]
        def phase_on():
            phn[0] += 1
            return phn[0] <= phlim
        if phase_on():
            self.ph_init()
        if phase_on():
            self.ph_wcast_outer()
        if phase_on():
            with self.scope():
                self.ph_xn()
        if phase_on():
            self.ph_wcast_layer(0)
        if phase_on():
            with self.scope():
                self.ph_outer_qkv()
        if phase_on():
            with self.scope():
                self.ph_outer_attn()
        if phase_on():
            with self.scope():
                self.ph_outer_wos()
        ybufs = {0: self.y0, 1: self.yA, 2: self.yB}
        cur = 0
        for l in range(CFG.DEPTH):
            if l + 1 < CFG.DEPTH and phase_on():
                self.ph_wcast_layer(l + 1)
            ysrc = ybufs[cur]
            ydst = self.yA if ysrc is not self.yA else self.yB
            if phase_on():
                with self.scope():
                    self.ph_inner_attn(l, ysrc, self.yM)
            if phase_on():
                with self.scope():
                    self.ph_inner_ffn_main(l, self.yM, ydst)
            cur = 1 if ydst is self.yA else 2
        if phase_on():
            with self.scope():
                self.ph_final(ybufs[cur])
        p.wait("sync", [Tok(s, v) for s, v in p.cnt.items() if v > 0])

    def ph_init(self):
        p = self.p
        for sbuf, d in [(self.c_id32, self.d_ident32), (self.c_idbf, self.d_identbf),
                        (self.c_ones1x, self.d_ones_1x128), (self.c_ones128, self.d_ones_128x1),
                        (self.c_rperm, self.d_rperm), (self.c_maskT, self.d_maskT),
                        (self.c_maskW0, self.d_maskW0)]:
            self.B(sbuf).wrote(p.dma("qL", sbuf[:, :], d.ap(), deps=[]))
        o65 = p.op("dve", lambda e: e.memset(self.c_ones65[:, :], 1.0), deps=[])
        self.B(self.c_ones65).wrote(o65)
        for cc in range(CFG.CH):
            t1 = p.dma("qL", self.c_esel[:, cc * 16:(cc + 1) * 16], self.d_esel.ap()[cc], deps=[])
            t2 = p.dma("qL", self.c_eprime[0:16, cc * 128:(cc + 1) * 128], self.d_eprime.ap()[cc], deps=[])
        self.B(self.c_esel).wrote(t1)
        self.B(self.c_eprime).wrote(t2)
        for l in range(CFG.DEPTH):
            for cc in range(CFG.CH):
                tok = p.dma("qL", self.b2_sb[:, l * CFG.CH + cc:l * CFG.CH + cc + 1],
                            self.b2_in.ap()[l, cc * 128:(cc + 1) * 128].rearrange("(f o) -> f o", o=1),
                            deps=[], allow_slow_non_contiguous=True)
        self.B(self.b2_sb).wrote(tok)
        lnb32 = self.sb("lnb32", [128, CFG.DEPTH * CFG.CH], F32)
        for l in range(CFG.DEPTH):
            for cc in range(CFG.CH):
                tok = p.dma("qL", lnb32[:, l * CFG.CH + cc:l * CFG.CH + cc + 1],
                            self.lnb_in.ap()[l, cc * 128:(cc + 1) * 128].rearrange("(f o) -> f o", o=1),
                            deps=[], allow_slow_non_contiguous=True)
        self.B(lnb32).wrote(tok)
        self.lnb_sb = lnb32
        for l in range(CFG.DEPTH):
            for cc in range(CFG.CH):
                tok = p.dma("qL", self.lng_sb[:, l * CFG.CH + cc:l * CFG.CH + cc + 1],
                            self.lng_in.ap()[l, cc * 128:(cc + 1) * 128].rearrange("(f o) -> f o", o=1),
                            deps=[], allow_slow_non_contiguous=True)
        self.B(self.lng_sb).wrote(tok)
        for l in range(CFG.DEPTH):
            for cc in range(44):
                tok = p.dma("qL", self.b1p_sb[:, l * 44 + cc:l * 44 + cc + 1],
                            self.b1p_in.ap()[l, cc * 128:(cc + 1) * 128].rearrange("(f o) -> f o", o=1),
                            deps=[], allow_slow_non_contiguous=True)
        self.B(self.b1p_sb).wrote(tok)
        lastq = {}
        for l in range(CFG.DEPTH):
            for (dst, src, perm) in ((self.qs8_sb, self.qs_in, False), (self.qs8p_sb, self.qs_in, True),
                                     (self.ks_sb, self.ks_in, False), (self.ksp_sb, self.ks_in, True)):
                for hb in range(2):
                    base = hb * 64
                    if not perm:
                        tok = p.dma("qL", dst[base:base + 64, l:l + 1],
                                    src.ap()[l, 0:64].rearrange("(f o) -> f o", o=1), deps=[],
                                    allow_slow_non_contiguous=True)
                    else:
                        tok = p.dma("qL", dst[base:base + 32, l:l + 1],
                                    src.ap()[l, 32:64].rearrange("(f o) -> f o", o=1), deps=[],
                                    allow_slow_non_contiguous=True)
                        tok = p.dma("qL", dst[base + 32:base + 64, l:l + 1],
                                    src.ap()[l, 0:32].rearrange("(f o) -> f o", o=1), deps=[],
                                    allow_slow_non_contiguous=True)
                    lastq[dst.name] = tok
        t1 = p.op("dve", lambda e: e.tensor_scalar_mul(self.qs8_sb[:, :], self.qs8_sb[:, :], 8.0),
                  deps=[lastq[self.qs8_sb.name]])
        t2 = p.op("dve", lambda e: e.tensor_scalar_mul(self.qs8p_sb[:, :], self.qs8p_sb[:, :], 8.0),
                  deps=[t1, lastq[self.qs8p_sb.name]])
        self.B(self.qs8_sb).wrote(t1)
        self.B(self.qs8p_sb).wrote(t2)
        self.B(self.ks_sb).wrote(lastq[self.ks_sb.name])
        self.B(self.ksp_sb).wrote(lastq[self.ksp_sb.name])
        tiny = self.sb("tiny", [1, 256], F32)
        tinyb = self.sb("tinyb", [1, 8], BF16)
        for l in range(CFG.DEPTH):
            ta = p.dma("qL", tiny[0:1, 0:64], self.qs_in.ap()[l].rearrange("(o f) -> o f", o=1),
                       deps=self.B(tiny).wdeps())
            tb = p.dma("qL", tiny[0:1, 64:128], self.ks_in.ap()[l].rearrange("(o f) -> o f", o=1), deps=[])
            u1 = p.op("dve", lambda e: e.reduce_max(tiny[0:1, 128:129], tiny[0:1, 0:64], axis=AX.X,
                                                    apply_absolute_value=True), deps=[ta, tb])
            u2 = p.op("dve", lambda e: e.reduce_max(tiny[0:1, 129:130], tiny[0:1, 64:128], axis=AX.X,
                                                    apply_absolute_value=True), deps=[u1])
            u3 = p.op("dve", lambda e: e.tensor_mul(tiny[0:1, 130:131], tiny[0:1, 128:129],
                                                    tiny[0:1, 129:130]), deps=[u2])
            u4 = p.op("dve", lambda e: e.tensor_scalar_mul(tiny[0:1, 131:132], tiny[0:1, 130:131], -8.0),
                      deps=[u3])
            u5 = p.op("dve", lambda e, l=l: e.tensor_copy(tinyb[0:1, l:l + 1], tiny[0:1, 131:132]), deps=[u4])
            self.B(tiny).wrote(u5)
            mm = p.op("pe", lambda e, l=l: e.matmul(self.ps_m[:, 0:1], self.c_ones1x[0:1, :],
                                                    tinyb[0:1, l:l + 1], start=True, stop=True),
                      deps=[u5, self.B(self.c_ones1x).ready1] + self.pwd(self.ps_m, (0,)))
            self.pwrote0(self.ps_m, mm)
            ev = p.op("dve", lambda e, l=l: e.tensor_copy(self.negc_sb[:, l:l + 1], self.ps_m[:, 0:1]),
                      deps=[mm])
            self.pread0(self.ps_m, ev)
        self.B(self.negc_sb).wrote(ev)

    def _wcast_piece(self, src_ap, F, dst_aps, npart=128, pads=(), srcw=None):
        p = self.p
        w16 = self.w16
        b16 = self.B(w16)
        SW = srcw or F
        CW = 1536
        last = None
        for c0 in range(0, F, CW):
            c1 = min(F, c0 + CW)
            buf = self.wch[self.wch_par % 2]
            self.wch_par += 1
            bb = self.B(buf)
            deps = bb.wdeps()
            if npart < 128 or c1 > SW:
                ms = p.op("pool", lambda e, buf=buf, c0=c0, c1=c1: e.memset(buf[:, 0:c1 - c0], 0.0),
                          deps=deps)
                deps = [ms]
            s1 = min(SW, c1)
            ld = p.dma("qW", buf[0:npart, 0:s1 - c0], src_ap[:, c0:s1], deps=deps)
            bb.wrote(ld)
            ct = p.op("pool", lambda e, buf=buf, c0=c0, c1=c1: e.tensor_copy(
                w16[:, c0:c1], buf[:, 0:c1 - c0]), deps=[ld] + (b16.wdeps() if c0 == 0 else []))
            bb.read(ct)
            last = ct
        b16.wrote(last)
        for (dap, c0, wd) in dst_aps:
            sap = w16[:, c0:c0 + wd]
            if wd % 512 == 0 and wd > 512:
                sap = sap.rearrange("p (g j) -> p g j", j=512)
            st = p.dma("qX", dap, sap, deps=[last])
            b16.read(st)
        return st

    def ph_wcast_outer(self):
        p = self.p
        D = CFG.D
        for (src, dst) in ((self.wq_in, self.wq_bf), (self.wk_in, self.wk_bf),
                           (self.wv_in, self.wv_bf), (self.woo_in, self.woo_bf)):
            for fc in range(CFG.CH):
                dstv = dst.ap()[fc].rearrange("g p j -> p g j")
                st = self._wcast_piece(src.ap()[fc * 128:(fc + 1) * 128, :], D, [(dstv, 0, D)])
            self.B(dst).wrote_add(st)

    def ph_wcast_layer(self, l):
        p = self.p
        D, DI, F1H = CFG.D, CFG.DI, CFG.F1H
        for fc in range(CFG.CH):
            dstv = self.wqkv_bf.ap()[l, fc].rearrange("g p j -> p g j")
            st = self._wcast_piece(self.wqkv_in.ap()[l, fc * 128:(fc + 1) * 128, :], 3 * D, [(dstv, 0, 3 * D)])
        self.B(self.wqkv_bf).wrote_add(st)
        for fc in range(CFG.CH):
            dstv = self.wo_bf.ap()[l, fc].rearrange("g p j -> p g j")
            st = self._wcast_piece(self.wo_in.ap()[l, fc * 128:(fc + 1) * 128, :], D, [(dstv, 0, D)])
        self.B(self.wo_bf).wrote_add(st)
        for fc in range(CFG.CH):
            for half in range(2):
                w1v = self.w1_bf.ap()[l, fc].rearrange("g p j -> p g j")
                if half == 0:
                    dst_aps = [(w1v[:, 0:5, :], 0, 2560), (w1v[:, 5, 0:256], 2560, 256)]
                else:
                    dst_aps = [(w1v[:, 5, 256:512], 0, 256), (w1v[:, 6:11, :], 256, 2560)]
                st = self._wcast_piece(
                    self.w1_in.ap()[l, fc * 128:(fc + 1) * 128, half * DI:(half + 1) * DI],
                    F1H, dst_aps, pads=(1,), srcw=DI)
        self.B(self.w1_bf).wrote_add(st)
        for fc in range(CFG.NC2):
            r0 = fc * 128
            r1 = min(DI, r0 + 128)
            dstv = self.w2_bf.ap()[l, fc].rearrange("g p j -> p g j")
            st = self._wcast_piece(self.w2_in.ap()[l, r0:r1, :], D, [(dstv, 0, D)], npart=r1 - r0)
        self.B(self.w2_bf).wrote_add(st)


    def _ln_rows(self, stt, z2s, src, dep):
        p = self.p
        s1 = p.op("dve", lambda e: e.reduce_sum(stt[:, 0:1], src[:, :], axis=AX.X),
                  deps=[dep] + self.B(stt).wdeps())
        sq = p.op("act", lambda e: e.activation(z2s[:, :], src[:, :], AF.Square),
                  deps=[dep] + self.B(z2s).wdeps())
        self.B(z2s).wrote(sq)
        s2 = p.op("dve", lambda e: e.reduce_sum(stt[:, 1:2], z2s[:, :], axis=AX.X), deps=[sq, s1])
        self.B(z2s).read(s2)
        mu = p.op("dve", lambda e: e.tensor_scalar_mul(stt[:, 2:3], stt[:, 0:1], 1.0 / CFG.D), deps=[s2])
        m2 = p.op("dve", lambda e: e.tensor_mul(stt[:, 3:4], stt[:, 2:3], stt[:, 2:3]), deps=[mu])
        var = p.op("dve", lambda e: e.scalar_tensor_tensor(stt[:, 4:5], stt[:, 1:2], 1.0 / CFG.D,
                                                           stt[:, 3:4], ALU.mult, ALU.subtract), deps=[m2])
        ve = p.op("dve", lambda e: e.tensor_scalar_add(stt[:, 4:5], stt[:, 4:5], CFG.LN_EPS), deps=[var])
        ln = p.op("act", lambda e: e.activation(stt[:, 5:6], stt[:, 4:5], AF.Ln), deps=[ve])
        rs = p.op("act", lambda e: e.activation(stt[:, 6:7], stt[:, 5:6], AF.Exp, scale=-0.5), deps=[ln])
        nm = p.op("dve", lambda e: e.tensor_mul(stt[:, 7:8], stt[:, 2:3], stt[:, 6:7]), deps=[rs])
        nm2 = p.op("dve", lambda e: e.tensor_scalar_mul(stt[:, 7:8], stt[:, 7:8], -1.0), deps=[nm])
        self.B(stt).wrote(nm2)
        return nm2

    def ph_xn(self):
        p, S = self.p, self.S
        xt = self.sb("xt", [128, 1024], F32)
        z2s = self.sb("xz2", [128, 1024], F32)
        xnt = self.sb("xnt", [128, 1024], F32)
        xnb = self.sb("xnb", [128, 1024], BF16)
        xfm = self.sb("xfm", [128, 1024], F32)
        xfb = self.sb("xfb", [128, 1024], BF16)
        stt = self.sb("xstats", [128, 8], F32)
        for tt in range(S // 128):
            ld = p.dma("qL", xt[:, :], self.x_in.ap()[tt * 128:(tt + 1) * 128, :], deps=self.B(xt).wdeps())
            self.B(xt).wrote(ld)
            st = self._ln_rows(stt, z2s, xt, ld)
            ap = p.op("dve", lambda e: e.tensor_scalar(xnt[:, :], xt[:, :], stt[:, 6:7], stt[:, 7:8],
                                                       ALU.mult, ALU.add), deps=[st, ld] + self.B(xnt).wdeps())
            self.B(stt).read(ap)
            self.B(xnt).wrote(ap)
            self.B(xt).read(ap)
            std = p.dma("qS", self.xn_tm.ap()[tt], xnt[:, :], deps=[ap])
            self.B(xnt).read(std)
            self.B(self.xn_tm).wrote_add(std)
            cb = p.op("dve", lambda e: e.tensor_copy(xnb[:, :], xnt[:, :]), deps=[ap] + self.B(xnb).wdeps())
            self.B(xnb).wrote(cb)
            self.B(xnt).read(cb)
            for k in range(8):
                tk = p.op("pe", lambda e, k=k: e.transpose(self.ps_t[:, k * 128:(k + 1) * 128],
                                                           xnb[:, k * 128:(k + 1) * 128], self.c_idbf[:, :]),
                          deps=[cb, self.B(self.c_idbf).ready1] +
                               (self.pwd(self.ps_t, (0,)) if k == 0 else []), inc=(k == 7))
            self.pwrote0(self.ps_t, tk)
            self.B(xnb).read(tk)
            ev = p.op("dve", lambda e: e.tensor_copy(xfb[:, :], self.ps_t[:, :]),
                      deps=[tk] + self.B(xfb).wdeps())
            self.pread0(self.ps_t, ev)
            self.B(xfb).wrote(ev)
            for cc in range(8):
                dst = self.xn_bf.ap()[cc, tt // 4, :, (tt % 4) * 128:(tt % 4) * 128 + 128]
                dtok = p.dma("qS", dst, xfb[:, cc * 128:(cc + 1) * 128], deps=[ev])
                self.B(xfb).read(dtok)
                self.B(self.xn_bf).wrote_add(dtok)
            for k in range(8):
                tk = p.op("pe", lambda e, k=k: e.transpose(self.ps_a[:, k * 128:(k + 1) * 128],
                                                           xnt[:, k * 128:(k + 1) * 128], self.c_id32[:, :]),
                          deps=[ap, self.B(self.c_id32).ready1] +
                               (self.pwd(self.ps_a) if k == 0 else []), inc=(k == 7))
            self.pwrote(self.ps_a, tk)
            self.B(xnt).read(tk)
            ev2 = p.op("dve", lambda e: e.tensor_copy(xfm[:, :], self.ps_a[:, :]),
                       deps=[tk] + self.B(xfm).wdeps())
            self.pread(self.ps_a, ev2)
            self.B(xfm).wrote(ev2)
            for cc in range(8):
                dtok2 = p.dma("qS", self.y0.ap()[cc, :, tt * 128:(tt + 1) * 128],
                              xfm[:, cc * 128:(cc + 1) * 128], deps=[ev2])
                self.B(xfm).read(dtok2)
                self.B(self.y0).wrote_add(dtok2)


    def ph_outer_qkv(self):
        p, S = self.p, self.S
        xsb = [self.sb("oq_x0", [128, 8 * 512], BF16), self.sb("oq_x1", [128, 8 * 512], BF16)]
        wq_sb = self.sb("oq_wq", [128, 16 * 512], BF16)
        wk_sb = self.sb("oq_wk", [128, 16 * 512], BF16)
        wv_sb = self.sb("oq_wv", [128, 16 * 512], BF16)
        ev = [self.sb(f"oq_ev{i}", [128, 512], BF16) for i in range(4)]
        vtmp = [self.sb("oq_vt0", [128, 1040], BF16), self.sb("oq_vt1", [128, 1040], BF16)]
        for vt in vtmp:
            ms = p.op("dve", lambda e, vt=vt: e.memset(
                vt[:, :].rearrange("p (h j) -> p h j", j=65)[:, :, 64:65], 1.0), deps=[])
            self.B(vt).wrote(ms)
        wtoks = {}
        for (wt, wsb) in ((self.wq_bf, wq_sb), (self.wk_bf, wk_sb), (self.wv_bf, wv_sb)):
            for fc in range(8):
                for g in range(2):
                    tok = p.dma("qV", wsb[:, (g * 8 + fc) * 512:(g * 8 + fc + 1) * 512],
                                wt.ap()[fc, g], deps=[] if (fc == 0 and g == 0) else [])
                    self.B(wt).read(tok)
            self.B(wsb).wrote(tok)
            wtoks[wsb.name] = tok
        for qb in range(S // 512):
            xb = xsb[qb % 2]
            xdeps = self.B(xb).wdeps() + self.B(self.xn_bf).rdeps()
            for cc in range(8):
                xtok = p.dma("qL", xb[:, cc * 512:(cc + 1) * 512], self.xn_bf.ap()[cc, qb], deps=xdeps)
                self.B(self.xn_bf).read(xtok)
            self.B(xb).wrote(xtok)
            for (wsb, dstd, isq) in ((wq_sb, self.q_d, True), (wk_sb, self.k_d, False)):
                wtok = wtoks[wsb.name]
                for co in range(8):
                    g, jj = co // 4, co % 4
                    reg = co % 2
                    pd = self.pwd(self.ps_a, (reg,))
                    for fc in range(8):
                        mm = p.op("pe", lambda e, fc=fc, g=g, jj=jj, reg=reg, wsb=wsb, xb=xb: e.matmul(
                            self.ps_a[:, reg * 512:(reg + 1) * 512],
                            wsb[:, (g * 8 + fc) * 512 + jj * 128:(g * 8 + fc) * 512 + jj * 128 + 128],
                            xb[:, fc * 512:(fc + 1) * 512], start=(fc == 0), stop=(fc == 7)),
                            deps=[wtok, xtok] + (pd if fc == 0 else []), inc=(fc == 7))
                    self.pwrote(self.ps_a, mm, (reg,))
                    self.B(wsb).read(mm)
                    self.B(xb).read(mm)
                    eb = ev[co % 4]
                    if isq:
                        cv = p.op("dve", lambda e, reg=reg, eb=eb: e.tensor_scalar_mul(
                            eb[:, :], self.ps_a[:, reg * 512:(reg + 1) * 512], 0.125),
                            deps=[mm] + self.B(eb).wdeps())
                    else:
                        cv = p.op("dve", lambda e, reg=reg, eb=eb: e.tensor_copy(
                            eb[:, :], self.ps_a[:, reg * 512:(reg + 1) * 512]),
                            deps=[mm] + self.B(eb).wdeps())
                    self.pread(self.ps_a, cv, (reg,))
                    self.B(eb).wrote(cv)
                    d1 = p.dma("qS", dstd.ap()[2 * co, qb], eb[0:64, :], deps=[cv])
                    d2 = p.dma("qS", dstd.ap()[2 * co + 1, qb], eb[64:128, :], deps=[cv])
                    self.B(eb).read(d1)
                    self.B(eb).read(d2)
                    self.B(dstd).wrote_add(d2)
            wtok = wtoks[wv_sb.name]
            for tl in range(4):
                tt = qb * 4 + tl
                vt = vtmp[tl % 2]
                for g in range(2):
                    pd = self.pwd(self.ps_b, (g,))
                    for fc in range(8):
                        mm = p.op("pe", lambda e, fc=fc, g=g, tl=tl, xb=xb: e.matmul(
                            self.ps_b[:, g * 512:(g + 1) * 512],
                            xb[:, fc * 512 + tl * 128:fc * 512 + tl * 128 + 128],
                            wv_sb[:, (g * 8 + fc) * 512:(g * 8 + fc + 1) * 512],
                            start=(fc == 0), stop=(fc == 7)),
                            deps=[wtok, xtok] + (pd if fc == 0 else []), inc=(fc == 7))
                    self.pwrote(self.ps_b, mm, (g,))
                    self.B(wv_sb).read(mm)
                    self.B(xb).read(mm)
                    cv = p.op("dve", lambda e, g=g, vt=vt: e.tensor_copy(
                        vt[:, :].rearrange("p (h j) -> p h j", j=65)[:, g * 8:(g + 1) * 8, 0:64],
                        self.ps_b[:, g * 512:(g + 1) * 512]), deps=[mm] + self.B(vt).wdeps())
                    self.pread(self.ps_b, cv, (g,))
                    self.B(vt).wrote(cv)
                dv = p.dma("qS", self.v_d.ap()[tt], vt[:, :], deps=[cv])
                self.B(vt).read(dv)
                self.B(self.v_d).wrote_add(dv)

    def ph_outer_attn(self):
        p, S = self.p, self.S
        NKT = S // 128
        QB = CFG.QB
        NQB = S // QB
        NS = S // CFG.SAMPLE
        k_sb = self.sb("oa_k", [65, 4 * S], BF16)
        v_sb = self.sb("oa_v", [128, NKT, 260], BF16)
        q_sb = self.sb("oa_q", [65, 4 * QB], BF16)
        ksub = self.sb("oa_ksub", [64, 4 * NS], BF16)
        pt = [self.sb("oa_pt0", [128, QB], BF16), self.sb("oa_pt1", [128, QB], BF16)]
        mneg = self.sb("oa_mneg", [128, QB // 128], F32)
        mnegb = self.sb("oa_mnegb", [QB // 128, 128], BF16)
        rrow = self.sb("oa_rrow", [1, QB], BF16)
        rrow32 = self.sb("oa_rrow32", [1, QB], F32)
        ostage = self.sb("oa_ost", [65, QB], F32)
        bcb = self.sb("oa_bcb", [64, QB], BF16)
        o2 = self.sb("oa_o2", [128, QB], BF16)
        ones_ms = p.op("dve", lambda e: e.memset(k_sb[64:65, :], 1.0), deps=[])
        self.B(k_sb).wrote(ones_ms)
        for hg in range(CFG.H // 4):
            vdeps = self.B(v_sb).wdeps() + self.B(self.v_d).rdeps()
            for tt in range(NKT):
                vtok = p.dma("qL", v_sb[:, tt, :], self.v_d.ap()[tt][:, hg * 260:(hg + 1) * 260], deps=vdeps)
                self.B(self.v_d).read(vtok)
            self.B(v_sb).wrote(vtok)
            kdeps = self.B(k_sb).wdeps() + self.B(self.k_d).rdeps() + [ones_ms]
            for hh in range(4):
                h = hg * 4 + hh
                for qq in range(S // 512):
                    ktok = p.dma("qL", k_sb[0:64, hh * S + qq * 512:hh * S + (qq + 1) * 512],
                                 self.k_d.ap()[h, qq], deps=kdeps)
                    self.B(self.k_d).read(ktok)
            self.B(k_sb).wrote(ktok)
            for hh in range(4):
                kstok = p.op("dve", lambda e, hh=hh: e.tensor_copy(
                    ksub[:, hh * NS:(hh + 1) * NS],
                    k_sb[0:64, hh * S:(hh + 1) * S].rearrange("p (n s) -> p n s", s=CFG.SAMPLE)
                        [:, :, 0:1].rearrange("p n o -> p (n o)")),
                    deps=[ktok] + (self.B(ksub).wdeps() if hh == 0 else []))
            self.B(ksub).wrote(kstok)
            for qb in range(NQB):
                qdeps = self.B(q_sb).wdeps() + self.B(self.q_d).rdeps()
                for hh in range(4):
                    h = hg * 4 + hh
                    for j in range(QB // 512):
                        qtok = p.dma("qL", q_sb[0:64, hh * QB + j * 512:hh * QB + (j + 1) * 512],
                                     self.q_d.ap()[h, qb * (QB // 512) + j], deps=qdeps)
                        self.B(self.q_d).read(qtok)
                self.B(q_sb).wrote(qtok)
                for hh in range(4):
                    h = hg * 4 + hh
                    nqt = QB // 128
                    for ii in range(nqt):
                        mm = p.op("pe", lambda e, ii=ii, hh=hh: e.matmul(
                            self.ps_m[:, 0:NS],
                            q_sb[0:64, hh * QB + ii * 128:hh * QB + ii * 128 + 128],
                            ksub[:, hh * NS:(hh + 1) * NS], start=True, stop=True),
                            deps=[qtok, kstok] + self.pwd(self.ps_m, (0,)), inc=True)
                        self.pwrote0(self.ps_m, mm)
                        rd = p.op("dve", lambda e, ii=ii: e.reduce_max(
                            mneg[:, ii:ii + 1], self.ps_m[:, 0:NS], axis=AX.X, negate=True),
                            deps=[mm] + (self.B(mneg).wdeps() if ii == 0 else []))
                        self.pread0(self.ps_m, rd)
                    self.B(mneg).wrote(rd)
                    tr = p.op("pe", lambda e: e.transpose(self.ps_a[0:nqt, 0:128], mneg[:, 0:nqt],
                                                          self.c_id32[:, :]),
                              deps=[rd, self.B(self.c_id32).ready1] + self.pwd(self.ps_a), inc=True)
                    self.pwrote(self.ps_a, tr)
                    self.B(mneg).read(tr)
                    cv = p.op("dve", lambda e: e.tensor_copy(mnegb[:, :], self.ps_a[0:nqt, 0:128]),
                              deps=[tr] + self.B(mnegb).wdeps())
                    self.pread(self.ps_a, cv)
                    self.B(mnegb).wrote(cv)
                    dm = p.dma("qG", q_sb[64:65, hh * QB:(hh + 1) * QB].rearrange("p (a b) -> p a b", b=128),
                               mnegb[:, :], deps=[cv])
                    self.B(mnegb).read(dm)
                    self.B(q_sb).wrote(dm)
                    qtok2 = dm
                    simbufs = [self.ps_a, self.ps_b]
                    avdeps = self.pwd(self.ps_c)
                    av_last = None
                    exs = {}
                    for kt in range(NKT + 1):
                        if kt < NKT:
                            sel = kt % 2
                            ps = simbufs[sel]
                            sdeps = [qtok2, ktok, vtok] + self.pwd(ps)
                            for j in range(2):
                                mm = p.op("pe", lambda e, j=j, kt=kt, hh=hh, ps=ps: e.matmul(
                                    ps[:, j * 512:(j + 1) * 512],
                                    k_sb[0:65, hh * S + kt * 128:hh * S + kt * 128 + 128],
                                    q_sb[0:65, hh * QB + j * 512:hh * QB + (j + 1) * 512],
                                    start=True, stop=True),
                                    deps=(sdeps if j == 0 else []), inc=(j == 1))
                            self.pwrote(ps, mm)
                            ex = p.op("act", lambda e, sel=sel, ps=ps: e.activation(
                                pt[sel][:, :], ps[:, :], AF.Exp),
                                deps=[mm] + self.B(pt[sel]).wdeps())
                            self.pread(ps, ex)
                            self.B(pt[sel]).wrote(ex)
                            exs[kt] = ex
                        if kt > 0:
                            ka = kt - 1
                            sela = ka % 2
                            for j in range(2):
                                av = p.op("pe", lambda e, j=j, ka=ka, hh=hh, sela=sela: e.matmul(
                                    self.ps_c[0:65, j * 512:(j + 1) * 512],
                                    v_sb[:, ka, hh * 65:(hh + 1) * 65],
                                    pt[sela][:, j * 512:(j + 1) * 512],
                                    start=(ka == 0), stop=(ka == NKT - 1)),
                                    deps=([exs[ka]] + (avdeps if ka == 0 and j == 0 else [])),
                                    inc=(j == 1))
                            self.B(pt[sela]).read(av)
                            av_last = av
                    self.pwrote(self.ps_c, av_last)
                    self.B(q_sb).read(av_last)
                    self.B(k_sb).read(av_last)
                    self.B(v_sb).read(av_last)
                    self.B(ksub).read(av_last)
                    oev = p.op("dve", lambda e: e.tensor_copy(ostage[:, :], self.ps_c[0:65, 0:QB]),
                               deps=[av_last] + self.B(ostage).wdeps())
                    self.pread(self.ps_c, oev)
                    self.B(ostage).wrote(oev)
                    rc = p.op("act", lambda e: e.activation(rrow32[0:1, :], ostage[64:65, :], AF.Ln,
                                                            scale=2.0 ** -64),
                              deps=[oev] + self.B(rrow32).wdeps())
                    self.B(ostage).read(rc)
                    self.B(rrow32).wrote(rc)
                    rc = p.op("act", lambda e: e.activation(rrow[0:1, :], rrow32[0:1, :], AF.Exp,
                                                            scale=-1.0),
                              deps=[rc] + self.B(rrow).wdeps())
                    self.B(rrow32).read(rc)
                    self.B(rrow).wrote(rc)
                    for j in range(2):
                        bc = p.op("pe", lambda e, j=j: e.matmul(
                            self.ps_m[0:64, 0:512], self.c_ones1x[0:1, 0:64],
                            rrow[0:1, j * 512:(j + 1) * 512], start=True, stop=True),
                            deps=[rc] + self.pwd(self.ps_m, (0,)), inc=True)
                        self.pwrote0(self.ps_m, bc)
                        self.B(rrow).read(bc)
                        ec = p.op("dve", lambda e, j=j: e.tensor_copy(bcb[:, j * 512:(j + 1) * 512],
                                                                      self.ps_m[0:64, 0:512]),
                                  deps=[bc] + (self.B(bcb).wdeps() if j == 0 else []))
                        self.pread0(self.ps_m, ec)
                    self.B(bcb).wrote(ec)
                    row0 = 64 * (h % 2)
                    stx = p.op("dve", lambda e, row0=row0: e.scalar_tensor_tensor(
                        o2[row0:row0 + 64, :], ostage[0:64, :], 2.0 ** -64, bcb[:, :],
                        ALU.mult, ALU.mult),
                        deps=[ec, oev] + (self.B(o2).wdeps() if h % 2 == 0 else []))
                    self.B(ostage).read(stx)
                    self.B(bcb).read(stx)
                    self.B(o2).wrote(stx)
                    if h % 2 == 1:
                        cc = h // 2
                        for j in range(QB // 512):
                            d = p.dma("qS", self.o2_d.ap()[cc, qb * (QB // 512) + j],
                                      o2[:, j * 512:(j + 1) * 512], deps=[stx])
                            self.B(o2).read(d)
                            self.B(self.o2_d).wrote_add(d)

    def ph_outer_wos(self):
        p, S = self.p, self.S
        osb = self.sb("ow_o", [128, 8 * 512], BF16)
        wsb = self.sb("ow_w", [128, 16 * 512], BF16)
        aob = self.sb("ow_ao", [128, 8 * 512], BF16)
        ztm = self.sb("ow_z", [128, 1024], F32)
        z2s = self.sb("ow_z2", [128, 1024], F32)
        sout = self.sb("ow_s", [128, 1024], F32)
        stt = self.sb("ow_st", [128, 8], F32)
        xnt = self.sb("ow_xnt", [128, 1024], F32)
        cgam = self.sb("ow_gam", [128, CFG.D], F32)
        cbet = self.sb("ow_bet", [128, CFG.D], F32)
        self.B(cgam).wrote(p.dma("qL", cgam[:, :], self.gamma_in.ap(), deps=[]))
        self.B(cbet).wrote(p.dma("qL", cbet[:, :], self.beta_in.ap(), deps=[]))
        wdeps = self.B(wsb).wdeps() + self.B(self.woo_bf).rdeps()
        for fc in range(8):
            for g in range(2):
                wtok = p.dma("qV", wsb[:, (g * 8 + fc) * 512:(g * 8 + fc + 1) * 512],
                             self.woo_bf.ap()[fc, g], deps=wdeps)
                self.B(self.woo_bf).read(wtok)
        self.B(wsb).wrote(wtok)
        for qb in range(S // 512):
            odeps = self.B(osb).wdeps() + self.B(self.o2_d).rdeps()
            for cc in range(8):
                otok = p.dma("qL", osb[:, cc * 512:(cc + 1) * 512], self.o2_d.ap()[cc, qb], deps=odeps)
                self.B(self.o2_d).read(otok)
            self.B(osb).wrote(otok)
            for co in range(8):
                g, jj = co // 4, co % 4
                reg = co % 2
                pd = self.pwd(self.ps_a, (reg,))
                for fc in range(8):
                    mm = p.op("pe", lambda e, fc=fc, g=g, jj=jj, reg=reg: e.matmul(
                        self.ps_a[:, reg * 512:(reg + 1) * 512],
                        wsb[:, (g * 8 + fc) * 512 + jj * 128:(g * 8 + fc) * 512 + jj * 128 + 128],
                        osb[:, fc * 512:(fc + 1) * 512], start=(fc == 0), stop=(fc == 7)),
                        deps=[wtok, otok] + (pd if fc == 0 else []), inc=(fc == 7))
                self.pwrote(self.ps_a, mm, (reg,))
                self.B(wsb).read(mm)
                self.B(osb).read(mm)
                cv = p.op("dve", lambda e, co=co, reg=reg: e.tensor_copy(
                    aob[:, co * 512:(co + 1) * 512], self.ps_a[:, reg * 512:(reg + 1) * 512]),
                    deps=[mm] + (self.B(aob).wdeps() if co == 0 else []))
                self.pread(self.ps_a, cv, (reg,))
            self.B(aob).wrote(cv)
            aotok = cv
            for tl in range(4):
                tt = qb * 4 + tl
                pd = self.pwd(self.ps_t, (0,))
                for co in range(8):
                    tk = p.op("pe", lambda e, co=co, tl=tl: e.transpose(
                        self.ps_t[:, co * 128:(co + 1) * 128],
                        aob[:, co * 512 + tl * 128:co * 512 + tl * 128 + 128], self.c_idbf[:, :]),
                        deps=[aotok] + (pd if co == 0 else []), inc=(co == 7))
                self.pwrote0(self.ps_t, tk)
                self.B(aob).read(tk)
                lx = p.dma("qL", xnt[:, :], self.xn_tm.ap()[tt],
                           deps=self.B(xnt).wdeps() + self.B(self.xn_tm).rdeps())
                self.B(xnt).wrote(lx)
                self.B(self.xn_tm).read(lx)
                zt = p.op("dve", lambda e: e.scalar_tensor_tensor(ztm[:, :], self.ps_t[:, :], 1.0,
                                                                  xnt[:, :], ALU.bypass, ALU.add),
                          deps=[tk, lx] + self.B(ztm).wdeps())
                self.pread0(self.ps_t, zt)
                self.B(xnt).read(zt)
                self.B(ztm).wrote(zt)
                st = self._ln_rows(stt, z2s, ztm, zt)
                zn = p.op("dve", lambda e: e.tensor_scalar(sout[:, :], ztm[:, :], stt[:, 6:7], stt[:, 7:8],
                                                           ALU.mult, ALU.add),
                          deps=[st, zt] + self.B(sout).wdeps())
                self.B(stt).read(zn)
                self.B(ztm).read(zn)
                g1 = p.op("dve", lambda e: e.tensor_mul(sout[:, :], sout[:, :], cgam[:, :]),
                          deps=[zn, self.B(cgam).ready1])
                g2 = p.op("dve", lambda e: e.tensor_add(sout[:, :], sout[:, :], cbet[:, :]),
                          deps=[g1, self.B(cbet).ready1])
                self.B(sout).wrote(g2)
                d = p.dma("qS", self.s_tm.ap()[tt], sout[:, :], deps=[g2])
                self.B(sout).read(d)
                self.B(self.s_tm).wrote_add(d)

    def _shift_cast(self, ysrc, blk, xi_bf, ytile):
        p = self.p
        TB = CFG.TB
        t0 = blk * TB
        toks = []
        for cc in range(8):
            yb = self.B(ytile[cc % 2])
            ydeps = yb.wdeps()
            if blk > 0:
                ld = p.dma("qG", ytile[cc % 2][:, 0:TB + 1], ysrc.ap()[cc, :, t0 - 1:t0 + TB], deps=ydeps)
                self.B(ysrc).read(ld)
            else:
                ld = p.dma("qG", ytile[cc % 2][:, 1:TB + 1], ysrc.ap()[cc, :, 0:TB], deps=ydeps)
                self.B(ysrc).read(ld)
                ld = p.op("pool", lambda e, cc=cc: e.memset(ytile[cc % 2][:, 0:1], 0.0), deps=[ld])
            yb.wrote(ld)
            src = ytile[cc % 2][:, 1:TB + 1] if cc < 4 else ytile[cc % 2][:, 0:TB]
            ct = p.op("pool", lambda e, cc=cc, src=src: e.tensor_copy(xi_bf[:, cc * TB:(cc + 1) * TB], src),
                      deps=[ld] + (self.B(xi_bf).wdeps() if cc < 2 else []))
            yb.read(ct)
            toks.append(ct)
        self.B(xi_bf).wrote(toks[-1])
        return toks

    def _shift_dma1(self, ysrc, blk, cc, ytile):
        p = self.p
        TB = CFG.TB
        t0 = blk * TB
        yb = self.B(ytile[cc % 2])
        ld = p.dma("qG", ytile[cc % 2][:, 0:TB + 1], ysrc.ap()[cc, :, t0 - 1:t0 + TB],
                   deps=yb.wdeps())
        self.B(ysrc).read(ld)
        yb.wrote(ld)
        return ld

    def _shift_copy(self, cc, ld, xi_bf, ytile):
        p = self.p
        TB = CFG.TB
        yb = self.B(ytile[cc % 2])
        src = ytile[cc % 2][:, 1:TB + 1] if cc < 4 else ytile[cc % 2][:, 0:TB]
        ct = p.op("pool", lambda e, cc=cc, src=src: e.tensor_copy(
            xi_bf[:, cc * TB:(cc + 1) * TB], src),
            deps=[ld] + (self.B(xi_bf).wdeps() if cc < 2 else []))
        yb.read(ct)
        return ct

    def ph_inner_attn(self, l, ysrc, ydst):
        d = None
        p, S = self.p, self.S
        TB = CFG.TB
        NBLK = S // TB
        KH = TB + 256
        NSL = TB // 128 + 2
        sfx = f"_{l}"
        qbf = self.sb("ia_qbf" + sfx, [128, 8 * TB], BF16)
        qpre = self.sb("ia_qpre" + sfx, [128, 8 * TB], BF16)
        kfin = self.sb("ia_kfin" + sfx, [128, 8 * KH], BF16)
        vaug = self.sb("ia_vaug" + sfx, [128, NSL, 1040], BF16)
        rn = self.sb("ia_rn" + sfx, [16, TB], BF16)
        t1 = self.sb("ia_t1" + sfx, [128, TB], F32)
        t2 = self.sb("ia_t2" + sfx, [128, TB], F32)
        t4 = self.sb("ia_t4" + sfx, [16, 512], F32)
        pt = [self.sb("ia_pt0" + sfx, [128, TB], BF16), self.sb("ia_pt1" + sfx, [128, TB], BF16)]
        q2 = pt[0]
        sst = pt[1]
        tabs = self.sb("ia_tabs" + sfx, [128, 4 * TB], BF16)
        cst = self.sb("ia_cst" + sfx, [128, TB], BF16)
        wsbA = self.sb("ia_wA" + sfx, [128, 16 * 512], BF16)
        wsbB = self.sb("ia_wB" + sfx, [128, 16 * 512], BF16)
        rr = self.sb("ia_rr" + sfx, [1, TB], BF16)
        bcbu = self.sb("ia_bcb" + sfx, [64, TB], BF16)
        o2b = qbf
        xi, yt = self.ia_xi, self.ia_yt

        def load_w(wsb, pairs, srcd, l):
            deps = self.B(wsb).wdeps() + self.B(srcd).rdeps()
            for (slot, g) in pairs:
                for fc in range(8):
                    wt = p.dma("qV", wsb[:, (slot * 8 + fc) * 512:(slot * 8 + fc + 1) * 512],
                               srcd.ap()[l, fc, g], deps=deps)
                    self.B(srcd).read(wt)
            self.B(wsb).wrote(wt)
            return wt

        for blk in range(NBLK):
            t0 = blk * TB
            if blk == 0:
                hk = [p.op("pool", lambda e, cc=cc: e.memset(kfin[:, cc * KH:cc * KH + 256], 0.0),
                           deps=self.B(kfin).wdeps()) for cc in range(8)]
                hv = p.op("pool", lambda e: e.memset(vaug[:, 0:2, :], 0.0), deps=self.B(vaug).wdeps())
            else:
                hk = [p.op("pool", lambda e, cc=cc: e.tensor_copy(
                    kfin[:, cc * KH:cc * KH + 256], kfin[:, cc * KH + TB:cc * KH + TB + 256]),
                    deps=self.B(kfin).wdeps()) for cc in range(8)]
                hv = p.op("pool", lambda e: e.tensor_copy(vaug[:, 0:2, :], vaug[:, NSL - 2:NSL, :]),
                          deps=self.B(vaug).wdeps())
            tbuf = self.B(tabs)
            if blk == 0:
                xtoks = self._shift_cast(ysrc, blk, xi, yt)
                tdeps = tbuf.wdeps()
                for (i, dtab) in enumerate((self.d_tab_cs, self.d_tab_ss,
                                            self.d_tab_ci, self.d_tab_si)):
                    ttok = p.dma("qL", tabs[:, i * TB:(i + 1) * TB], dtab.ap()[:, t0:t0 + TB],
                                 deps=tdeps)
                tbuf.wrote(ttok)
            else:
                xtoks, ttok = next_xtoks, next_ttok
            xtok = xtoks[-1]
            for (is_q, svec, spvec, csoff, ssoff, slot_grps, wsb) in (
                    (True, self.qs8_sb, self.qs8p_sb, 0, TB, [(0, 0), (1, 1)], wsbA),
                    (False, self.ks_sb, self.ksp_sb, 2 * TB, 3 * TB, [(0, 2), (1, 3)], wsbB)):
                wtok = load_w(wsb, slot_grps, self.wqkv_bf, l)
                ssq_t = [self.ps_m, self.ps_c]
                ss_last = [None, None]
                for cc in range(8):
                    slot, jj = cc // 4, cc % 4
                    for j in range(2):
                        pd = self.pwd(self.ps_a, (j,))
                        for fc in range(8):
                            mm = p.op("pe", lambda e, fc=fc, slot=slot, jj=jj, j=j, wsb=wsb: e.matmul(
                                self.ps_a[:, j * 512:(j + 1) * 512],
                                wsb[:, (slot * 8 + fc) * 512 + jj * 128:(slot * 8 + fc) * 512 + jj * 128 + 128],
                                xi[:, fc * TB + j * 512:fc * TB + j * 512 + 512],
                                start=(fc == 0), stop=(fc == 7)),
                                deps=[wtok, xtok] + (pd if fc == 0 else []), inc=(fc == 7))
                        self.pwrote(self.ps_a, mm, (j,))
                        self.B(wsb).read(mm)
                        self.B(xi).read(mm)
                        cv = p.op("dve", lambda e, cc=cc, j=j: e.tensor_copy(
                            qbf[:, cc * TB + j * 512:cc * TB + j * 512 + 512],
                            self.ps_a[:, j * 512:(j + 1) * 512]),
                            deps=[mm] + (self.B(qbf).wdeps() if (cc == 0 and j == 0) else []))
                        self.pread(self.ps_a, cv, (j,))
                        sq = p.op("act", lambda e, cc=cc, j=j: e.activation(
                            q2[:, j * 512:(j + 1) * 512],
                            qbf[:, cc * TB + j * 512:cc * TB + j * 512 + 512], AF.Square),
                            deps=[cv] + self.B(q2).wdeps())
                        self.B(qbf).read(sq)
                        self.B(q2).wrote(sq)
                        ss = p.op("pe", lambda e, cc=cc, j=j: e.matmul(
                            ssq_t[j][0:16, 0:512], self.c_esel[:, cc * 16:(cc + 1) * 16],
                            q2[:, j * 512:(j + 1) * 512],
                            start=(cc == 0), stop=(cc == 7)),
                            deps=[sq, self.B(self.c_esel).ready1] +
                                 (self.pwd(ssq_t[j], (0,)) if cc == 0 else []), inc=True)
                        self.B(q2).read(ss)
                        ss_last[j] = ss
                self.B(qbf).wrote(cv)
                for j in range(2):
                    self.pwrote0(ssq_t[j], ss_last[j])
                    cl = p.op("dve", lambda e, j=j: e.tensor_scalar_max(
                        t4[0:16, 0:512], ssq_t[j][0:16, 0:512], 1e-24),
                        deps=[ss_last[j]] + self.B(t4).wdeps())
                    self.pread0(ssq_t[j], cl)
                    lnx = p.op("act", lambda e: e.activation(t4[0:16, 0:512], t4[0:16, 0:512], AF.Ln),
                               deps=[cl])
                    exx = p.op("act", lambda e, j=j: e.activation(rn[0:16, j * 512:(j + 1) * 512],
                                                                  t4[0:16, 0:512], AF.Exp, scale=-0.5),
                               deps=[lnx] + (self.B(rn).wdeps() if j == 0 else []))
                    self.B(t4).wrote(exx)
                    self.B(rn).wrote(exx)
                qbtok = self.B(qbf).ready1
                rntok = self.B(rn).ready1
                cq = p.op("dve", lambda e, svec=svec, csoff=csoff: e.tensor_scalar_mul(
                    cst[:, :], tabs[:, csoff:csoff + TB], svec[:, l:l + 1]),
                    deps=[ttok, self.B(svec).ready1] + self.B(cst).wdeps())
                tbuf.read(cq)
                self.B(cst).wrote(cq)
                sq2 = p.op("dve", lambda e, spvec=spvec, ssoff=ssoff: e.tensor_scalar_mul(
                    sst[:, :], tabs[:, ssoff:ssoff + TB], spvec[:, l:l + 1]),
                    deps=[ttok, self.B(spvec).ready1] + self.B(sst).wdeps())
                tbuf.read(sq2)
                self.B(sst).wrote(sq2)
                for cc in range(8):
                    pd = self.pwd(self.ps_b)
                    for j in range(2):
                        rq = p.op("pe", lambda e, cc=cc, j=j: e.matmul(
                            self.ps_b[:, j * 512:(j + 1) * 512], self.c_rperm[:, :],
                            qbf[:, cc * TB + j * 512:cc * TB + j * 512 + 512], start=True, stop=True),
                            deps=[qbtok, self.B(self.c_rperm).ready1] + (pd if j == 0 else []), inc=(j == 1))
                    self.pwrote(self.ps_b, rq)
                    self.B(qbf).read(rq)
                    pdm = self.pwd(self.ps_c)
                    for j in range(2):
                        rb = p.op("pe", lambda e, cc=cc, j=j: e.matmul(
                            self.ps_c[:, j * 512:(j + 1) * 512],
                            self.c_eprime[0:16, cc * 128:(cc + 1) * 128],
                            rn[0:16, j * 512:(j + 1) * 512], start=True, stop=True),
                            deps=[rntok, self.B(self.c_eprime).ready1] + (pdm if j == 0 else []), inc=(j == 1))
                    self.pwrote(self.ps_c, rb)
                    self.B(rn).read(rb)
                    p1 = p.op("pool", lambda e, cc=cc: e.tensor_mul(
                        t1[:, :], qbf[:, cc * TB:(cc + 1) * TB], cst[:, :]),
                        deps=[qbtok, cq] + self.B(t1).wdeps())
                    self.B(qbf).read(p1)
                    self.B(cst).read(p1)
                    self.B(t1).wrote(p1)
                    p2 = p.op("dve", lambda e: e.tensor_mul(
                        t2[:, :], self.ps_b[:, 0:TB], sst[:, :]),
                        deps=[rq, sq2] + self.B(t2).wdeps())
                    self.pread(self.ps_b, p2)
                    self.B(sst).read(p2)
                    self.B(t2).wrote(p2)
                    p3 = p.op("pool", lambda e: e.tensor_add(t1[:, :], t1[:, :], t2[:, :]),
                              deps=[p1, p2])
                    self.B(t2).read(p3)
                    self.B(t1).wrote(p3)
                    if is_q:
                        dsl = qpre[:, cc * TB:(cc + 1) * TB]
                        exdeps = self.B(qpre).wdeps() if cc == 0 else []
                    else:
                        dsl = kfin[:, cc * KH + 256:cc * KH + 256 + TB]
                        exdeps = [hk[cc]] + (self.B(kfin).wdeps() if cc == 0 else [])
                    p4 = p.op("dve", lambda e, dsl=dsl: e.tensor_mul(dsl, t1[:, :], self.ps_c[:, 0:TB]),
                              deps=[p3, rb] + exdeps)
                    self.B(t1).read(p4)
                    self.pread(self.ps_c, p4)
                if is_q:
                    self.B(qpre).wrote(p4)
                else:
                    self.B(kfin).wrote(p4)
            qtok = self.B(qpre).ready1
            ktok = self.B(kfin).ready1
            wsb = wsbA
            wtok = load_w(wsb, [(0, 4), (1, 5)], self.wqkv_bf, l)
            for tl in range(TB // 128):
                for g in range(2):
                    pd = self.pwd(self.ps_a, (g,))
                    for fc in range(8):
                        mm = p.op("pe", lambda e, fc=fc, g=g, tl=tl, wsb=wsb: e.matmul(
                            self.ps_a[:, g * 512:(g + 1) * 512],
                            xi[:, fc * TB + tl * 128:fc * TB + tl * 128 + 128],
                            wsb[:, (g * 8 + fc) * 512:(g * 8 + fc + 1) * 512],
                            start=(fc == 0), stop=(fc == 7)),
                            deps=[wtok, xtok] + (pd if fc == 0 else []), inc=(fc == 7))
                    self.pwrote(self.ps_a, mm, (g,))
                    self.B(wsb).read(mm)
                    self.B(xi).read(mm)
                    cv = p.op("dve", lambda e, tl=tl, g=g: e.tensor_copy(
                        vaug[:, tl + 2, :].rearrange("p (h j) -> p h j", j=65)[:, g * 8:(g + 1) * 8, 0:64],
                        self.ps_a[:, g * 512:(g + 1) * 512]),
                        deps=[mm, hv] + self.B(vaug).wdeps())
                    self.pread(self.ps_a, cv, (g,))
                    self.B(vaug).wrote(cv)
            on = p.op("pool", lambda e: e.memset(
                vaug[:, 2:NSL, :].rearrange("p t (h j) -> p t h j", j=65)[:, :, :, 64:65], 1.0),
                deps=[cv] + self.B(vaug).wdeps())
            self.B(vaug).wrote(on)
            vtok = on
            sh_lds = None
            if blk + 1 < NBLK:
                sh_lds = [self._shift_dma1(ysrc, blk + 1, cc, yt) for cc in range(2)]
                next_xtoks = []
                tdeps = tbuf.wdeps()
                for (i, dtab) in enumerate((self.d_tab_cs, self.d_tab_ss,
                                            self.d_tab_ci, self.d_tab_si)):
                    next_ttok = p.dma("qL", tabs[:, i * TB:(i + 1) * TB],
                                      dtab.ap()[:, t0 + TB:t0 + 2 * TB], deps=tdeps)
                tbuf.wrote(next_ttok)
            WPB = TB // CFG.WIN
            pend = [None]

            def emit_ln(pe):
                (h, evt, stg) = pe
                rc = p.op("act", lambda e, stg=stg: e.activation(stg[96:97, 0:TB], stg[64:65, 0:TB],
                                                                 AF.Ln, scale=2.0 ** -20),
                          deps=[evt])
                self.B(stg).read(rc)
                rc = p.op("act", lambda e, stg=stg: e.activation(rr[0:1, :], stg[96:97, 0:TB],
                                                                 AF.Exp, scale=-1.0),
                          deps=[rc] + self.B(rr).wdeps())
                self.B(stg).read(rc)
                self.B(rr).wrote(rc)
                return rc

            def emit_tail(pe, rc):
                (h, evt, stg) = pe
                cc, row = h // 2, 64 * (h % 2)
                for j in range(2):
                    bc = p.op("pe", lambda e, j=j: e.matmul(
                        self.ps_m[0:64, 0:512], self.c_ones1x[0:1, 0:64],
                        rr[0:1, j * 512:(j + 1) * 512], start=True, stop=True),
                        deps=[rc] + self.pwd(self.ps_m, (0,)), inc=True)
                    self.pwrote0(self.ps_m, bc)
                    self.B(rr).read(bc)
                    ecv = p.op("dve", lambda e, j=j: e.tensor_copy(
                        bcbu[:, j * 512:(j + 1) * 512], self.ps_m[0:64, 0:512]),
                        deps=[bc] + (self.B(bcbu).wdeps() if j == 0 else []))
                    self.pread0(self.ps_m, ecv)
                self.B(bcbu).wrote(ecv)
                stx = p.op("dve", lambda e, cc=cc, row=row, stg=stg: e.scalar_tensor_tensor(
                    o2b[row:row + 64, cc * TB:(cc + 1) * TB], stg[0:64, 0:TB], 2.0 ** -20,
                    bcbu[:, :], ALU.mult, ALU.mult),
                    deps=[ecv, evt] + (self.B(o2b).wdeps() if h == 0 else []))
                self.B(stg).read(stx)
                self.B(bcbu).read(stx)
                return stx

            for h in range(CFG.H):
                cc, row = h // 2, 64 * (h % 2)
                avdeps = self.pwd(self.ps_c)
                av_last = None
                mks = {}
                rc_prev = None
                for w in range(WPB + 1):
                    if w < WPB:
                        sel = w % 2
                        ps = [self.ps_a, self.ps_b][sel]
                        pd = self.pwd(ps)
                        for kt in range(4):
                            mm = p.op("pe", lambda e, kt=kt, w=w, cc=cc, row=row, ps=ps: e.matmul(
                                ps[:, kt * 256:(kt + 1) * 256],
                                kfin[row:row + 64, cc * KH + (2 * w + kt) * 128:cc * KH + (2 * w + kt) * 128 + 128],
                                qpre[row:row + 64, cc * TB + w * 256:cc * TB + w * 256 + 256],
                                start=True, stop=True),
                                deps=([qtok, ktok, vtok] + pd) if kt == 0 else [], inc=(kt == 3))
                        self.pwrote(ps, mm)
                        self.B(kfin).read(mm)
                        self.B(qpre).read(mm)
                        ex = p.op("act", lambda e, sel=sel, ps=ps: e.activation(
                            pt[sel][:, :], ps[:, :], AF.Exp, bias=self.negc_sb[:, l:l + 1]),
                            deps=[mm, self.B(self.negc_sb).ready1] + self.B(pt[sel]).wdeps())
                        self.pread(ps, ex)
                        mask = self.c_maskW0 if (blk == 0 and w == 0) else self.c_maskT
                        mk = p.op("dve", lambda e, sel=sel, mask=mask: e.tensor_mul(
                            pt[sel][:, :], pt[sel][:, :], mask[:, :]), deps=[ex, self.B(mask).ready1])
                        self.B(pt[sel]).wrote(mk)
                        mks[w] = mk
                    if w == 2 and pend[0] is not None:
                        rc_prev = emit_ln(pend[0])
                    if w > 0:
                        wa = w - 1
                        sela = wa % 2
                        for kt in range(4):
                            av = p.op("pe", lambda e, kt=kt, wa=wa, h=h, sela=sela: e.matmul(
                                self.ps_c[0:65, wa * 256:(wa + 1) * 256],
                                vaug[:, 2 * wa + kt, h * 65:(h + 1) * 65],
                                pt[sela][:, kt * 256:(kt + 1) * 256],
                                start=(kt == 0), stop=(kt == 3)),
                                deps=([mks[wa]] + (avdeps if (wa == 0 and kt == 0) else [])),
                                inc=(kt == 3))
                        self.B(pt[sela]).read(av)
                        self.B(vaug).read(av)
                        av_last = av
                if pend[0] is not None:
                    emit_tail(pend[0], rc_prev)
                self.pwrote(self.ps_c, av_last)
                stg = [t1, t2][h % 2]
                evt = p.op("dve", lambda e, stg=stg: e.tensor_copy(stg[0:65, 0:TB],
                                                                   self.ps_c[0:65, 0:TB]),
                           deps=[av_last] + self.B(stg).wdeps())
                self.pread(self.ps_c, evt)
                self.B(stg).wrote(evt)
                pend[0] = (h, evt, stg)
                if sh_lds is not None and h % 2 == 1:
                    cc2 = h // 2
                    next_xtoks.append(self._shift_copy(cc2, sh_lds[cc2], xi, yt))
                    if cc2 + 2 < 8:
                        sh_lds.append(self._shift_dma1(ysrc, blk + 1, cc2 + 2, yt))
                    if cc2 == 7:
                        self.B(xi).wrote(next_xtoks[-1])
            rc_prev = emit_ln(pend[0])
            stx = emit_tail(pend[0], rc_prev)
            self.B(o2b).wrote(stx)
            o2tok = stx
            wsb = wsbB
            wotok = load_w(wsb, [(0, 0), (1, 1)], self.wo_bf, l)
            for fo in range(8):
                slot, jj = fo // 4, fo % 4
                for j in range(2):
                    ysel = self.ia_yr[j % 2]
                    ly = p.dma("qG", ysel[:, :], ysrc.ap()[fo, :, t0 + j * 512:t0 + (j + 1) * 512],
                               deps=self.B(ysel).wdeps())
                    self.B(ysrc).read(ly)
                    self.B(ysel).wrote(ly)
                    pd = self.pwd(self.ps_b)
                    for fc in range(8):
                        mm = p.op("pe", lambda e, fc=fc, slot=slot, jj=jj, j=j, wsb=wsb: e.matmul(
                            self.ps_b[:, j * 512:(j + 1) * 512],
                            wsb[:, (slot * 8 + fc) * 512 + jj * 128:(slot * 8 + fc) * 512 + jj * 128 + 128],
                            o2b[:, fc * TB + j * 512:fc * TB + j * 512 + 512],
                            start=(fc == 0), stop=(fc == 7)),
                            deps=[wotok, o2tok] + (pd if fc == 0 else []), inc=(fc == 7))
                    self.pwrote(self.ps_b, mm)
                    self.B(wsb).read(mm)
                    self.B(o2b).read(mm)
                    rs = p.op("dve", lambda e, j=j, ysel=ysel: e.scalar_tensor_tensor(
                        self.ia_ym[:, :], self.ps_b[:, j * 512:(j + 1) * 512], 1.0, ysel[:, :],
                        ALU.bypass, ALU.add), deps=[mm, ly] + self.B(self.ia_ym).wdeps())
                    self.pread(self.ps_b, rs)
                    self.B(ysel).read(rs)
                    self.B(self.ia_ym).wrote(rs)
                    d = p.dma("qS", ydst.ap()[fo, :, t0 + j * 512:t0 + (j + 1) * 512], self.ia_ym[:, :],
                              deps=[rs])
                    self.B(self.ia_ym).read(d)
                    self.B(ydst).wrote_add(d)

    def ph_inner_ffn_main(self, l, ysrc, ydst):
        p, S = self.p, self.S
        TB = CFG.TB
        NCA = CFG.NCA
        sfx = f"_{l}"
        xin = self.ia_xi
        rbb = self.sb("fm_rb" + sfx, [128, 512], BF16)
        mbb = self.sb("fm_mb" + sfx, [128, 512], BF16)
        gbuf = self.sb("fm_g" + sfx, [128, TB], BF16)
        z = self.sb("fm_z" + sfx, [128, NCA * TB], BF16)
        w1a = [self.sb(f"fm_wa{i}" + sfx, [128, 8 * 512], BF16) for i in range(2)]
        w1g = [self.sb(f"fm_wg{i}" + sfx, [128, 8 * 512], BF16) for i in range(2)]
        w2s = [self.sb(f"fm_w2{i}" + sfx, [128, 22 * 512], BF16) for i in range(2)]
        rows = self.fs_rows
        xi = self.ia_xi

        aseq, gseq = [], []
        for co in range(NCA - 1, -1, -1):
            if not aseq or aseq[-1] != co // 4:
                aseq.append(co // 4)
            if not gseq or gseq[-1] != (co + NCA) // 4:
                gseq.append((co + NCA) // 4)

        def load_w1(bufs, g, slot):
            buf = bufs[slot]
            wd = self.B(buf).wdeps() + self.B(self.w1_bf).rdeps()
            for fc in range(8):
                wt = p.dma("qF", buf[:, fc * 512:(fc + 1) * 512], self.w1_bf.ap()[l, fc, g], deps=wd)
                self.B(self.w1_bf).read(wt)
            self.B(buf).wrote(wt)
            return wt

        def load_w2(g, slot):
            buf = w2s[slot]
            wd = self.B(buf).wdeps() + self.B(self.w2_bf).rdeps()
            for f2 in range(CFG.NC2):
                wt = p.dma("qF", buf[:, f2 * 512:(f2 + 1) * 512], self.w2_bf.ap()[l, f2, g], deps=wd)
                self.B(self.w2_bf).read(wt)
            self.B(buf).wrote(wt)
            return wt

        astate = {}
        gstate = {}

        def ensure(bufs, state, seq, idx):
            for k in (idx, idx + 1):
                if k < len(seq):
                    slot = k % 2
                    if state.get(slot) != seq[k]:
                        state[slot] = seq[k]
                        state[("tok", slot)] = load_w1(bufs, seq[k], slot)
            return bufs[idx % 2], state[("tok", idx % 2)]

        x2, rows2, tmp = self.fs_x2, self.fs_rows, self.fs_tmp
        for blk in range(S // TB):
            t0 = blk * TB
            xtoks = self._shift_cast(ysrc, blk, xi, self.ia_yt)
            xtok = xtoks[-1]
            for j in range(TB // 512):
                pd = self.pwd(self.ps_m, (0,))
                pc = self.pwd(self.ps_c, (0,))
                for cc in range(8):
                    sq = p.op("act", lambda e, cc=cc, j=j: e.activation(
                        x2[:, :], xi[:, cc * TB + j * 512:cc * TB + j * 512 + 512], AF.Square),
                        deps=[xtoks[cc]] + self.B(x2).wdeps())
                    self.B(x2).wrote(sq)
                    self.B(xi).read(sq)
                    m1 = p.op("pe", lambda e, cc=cc, j=j: e.matmul(
                        self.ps_m[0:1, 0:512], self.c_ones128[:, 0:1],
                        xi[:, cc * TB + j * 512:cc * TB + j * 512 + 512],
                        start=(cc == 0), stop=(cc == 7)),
                        deps=[xtok, self.B(self.c_ones128).ready1] + (pd if cc == 0 else []), inc=(cc == 7))
                    self.B(xi).read(m1)
                    m2 = p.op("pe", lambda e, cc=cc: e.matmul(
                        self.ps_c[0:1, 0:512], self.c_ones128[:, 0:1], x2[:, :],
                        start=(cc == 0), stop=(cc == 7)),
                        deps=[sq] + (pc if cc == 0 else []), inc=True)
                    self.B(x2).read(m2)
                self.pwrote0(self.ps_m, m1)
                self.pwrote0(self.ps_c, m2)
                td = self.B(tmp).wdeps()
                A, Bs = tmp[0:1, 0:512], tmp[0:1, 512:1024]
                C, Dm = Bs, A
                mu = p.op("dve", lambda e, A=A: e.tensor_scalar_mul(A, self.ps_m[0:1, 0:512],
                                                                   1.0 / CFG.D), deps=[m1] + td)
                self.pread0(self.ps_m, mu)
                mu2 = p.op("act", lambda e, A=A, Bs=Bs: e.activation(Bs, A, AF.Square), deps=[mu])
                var = p.op("dve", lambda e, Bs=Bs: e.scalar_tensor_tensor(
                    Bs, self.ps_c[0:1, 0:512], 1.0 / CFG.D, Bs,
                    ALU.mult, ALU.subtract), deps=[mu2, m2])
                self.pread0(self.ps_c, var)
                ve = p.op("dve", lambda e, Bs=Bs: e.tensor_scalar_add(Bs, Bs, CFG.LN_EPS), deps=[var])
                ln = p.op("act", lambda e, Bs=Bs: e.activation(Bs, Bs, AF.Ln), deps=[ve])
                rst = p.op("act", lambda e, Bs=Bs, C=C: e.activation(C, Bs, AF.Exp, scale=-0.5), deps=[ln])
                mur = p.op("dve", lambda e, A=A, C=C, Dm=Dm: e.tensor_mul(Dm, A, C), deps=[rst])
                self.B(tmp).wrote(mur)
                off = t0 + j * 512
                c1 = p.op("dve", lambda e, off=off, C=C: e.tensor_copy(rows2[0:1, off:off + 512], C),
                          deps=[mur] + self.B(rows2).wdeps())
                c2 = p.op("dve", lambda e, off=off, Dm=Dm: e.tensor_copy(rows2[64:65, off:off + 512], Dm),
                          deps=[mur, c1])
                self.B(tmp).read(c2)
                self.B(rows2).wrote(c2)
            for j in range(TB // 512):
                off = t0 + j * 512
                rdep = self.B(rows).rdeps()
                b1 = p.op("pe", lambda e, off=off: e.matmul(
                    self.ps_m[:, 0:512], self.c_ones1x[0:1, :], rows[0:1, off:off + 512],
                    start=True, stop=True), deps=rdep + self.pwd(self.ps_m, (0,)), inc=True)
                self.pwrote0(self.ps_m, b1)
                self.B(rows).read(b1)
                e1 = p.op("dve", lambda e: e.tensor_copy(rbb[:, :], self.ps_m[:, 0:512]),
                          deps=[b1] + self.B(rbb).wdeps())
                self.pread0(self.ps_m, e1)
                self.B(rbb).wrote(e1)
                b2 = p.op("pe", lambda e, off=off: e.matmul(
                    self.ps_m[:, 0:512], self.c_ones65[64:65, :], rows[64:65, off:off + 512],
                    start=True, stop=True), deps=rdep + [e1, self.B(self.c_ones65).ready1], inc=True)
                self.pwrote0(self.ps_m, b2)
                self.B(rows).read(b2)
                e2 = p.op("dve", lambda e: e.tensor_copy(mbb[:, :], self.ps_m[:, 0:512]),
                          deps=[b2] + self.B(mbb).wdeps())
                self.pread0(self.ps_m, e2)
                self.B(mbb).wrote(e2)
                for cc in range(8):
                    u1 = p.op("pool", lambda e, cc=cc, j=j: e.tensor_mul(
                        xin[:, cc * TB + j * 512:cc * TB + j * 512 + 512],
                        xi[:, cc * TB + j * 512:cc * TB + j * 512 + 512], rbb[:, :]),
                        deps=[xtok, e1] + (self.B(xin).wdeps() if cc == 0 and j == 0 else []))
                    self.B(xi).read(u1)
                    self.B(rbb).read(u1)
                    u2 = p.op("pool", lambda e, cc=cc, j=j: e.tensor_sub(
                        xin[:, cc * TB + j * 512:cc * TB + j * 512 + 512],
                        xin[:, cc * TB + j * 512:cc * TB + j * 512 + 512], mbb[:, :]),
                        deps=[u1, e2])
                    self.B(mbb).read(u2)
                    u2 = p.op("dve", lambda e, cc=cc, j=j: e.tensor_scalar(
                        xin[:, cc * TB + j * 512:cc * TB + j * 512 + 512],
                        xin[:, cc * TB + j * 512:cc * TB + j * 512 + 512],
                        self.lng_sb[:, l * CFG.CH + cc:l * CFG.CH + cc + 1],
                        self.lnb_sb[:, l * CFG.CH + cc:l * CFG.CH + cc + 1],
                        ALU.mult, ALU.add),
                        deps=[u2, self.B(self.lng_sb).ready1, self.B(self.lnb_sb).ready1])
            self.B(xin).wrote(u2)
            xntok = u2
            ai = gi = 0
            for idx, co in enumerate(range(NCA - 1, -1, -1)):
                cog = co + NCA
                if idx > 0 and aseq[ai] != co // 4:
                    ai += 1
                if idx > 0 and gseq[gi] != cog // 4:
                    gi += 1
                bufA, wtA = ensure(w1a, astate, aseq, ai)
                bufG, wtB = ensure(w1g, gstate, gseq, gi)
                jjA, jjB = co % 4, cog % 4
                for j in range(2):
                    pd = self.pwd(self.ps_a, (j,))
                    for fc in range(8):
                        mm = p.op("pe", lambda e, fc=fc, jjB=jjB, j=j, bufG=bufG: e.matmul(
                            self.ps_a[:, j * 512:(j + 1) * 512],
                            bufG[:, fc * 512 + jjB * 128:fc * 512 + jjB * 128 + 128],
                            xin[:, fc * TB + j * 512:fc * TB + j * 512 + 512],
                            start=(fc == 0), stop=(fc == 7)),
                            deps=[wtB, xntok] + (pd if fc == 0 else []), inc=(fc == 7))
                    self.pwrote(self.ps_a, mm, (j,))
                    self.B(xin).read(mm)
                    self.B(bufG).read(mm)
                    vcolg = self.b1p_sb[:, l * 44 + cog:l * 44 + cog + 1]
                    gl = p.op("act", lambda e, j=j, vcolg=vcolg: e.activation(
                        gbuf[:, j * 512:(j + 1) * 512], self.ps_a[:, j * 512:(j + 1) * 512],
                        AF.Gelu, bias=vcolg),
                        deps=[mm, self.B(self.b1p_sb).ready1] + (self.B(gbuf).wdeps() if j == 0 else []))
                    self.pread(self.ps_a, gl, (j,))
                    self.B(gbuf).wrote(gl)
                    pd = self.pwd(self.ps_b, (j,))
                    for fc in range(8):
                        mm2 = p.op("pe", lambda e, fc=fc, jjA=jjA, j=j, bufA=bufA: e.matmul(
                            self.ps_b[:, j * 512:(j + 1) * 512],
                            bufA[:, fc * 512 + jjA * 128:fc * 512 + jjA * 128 + 128],
                            xin[:, fc * TB + j * 512:fc * TB + j * 512 + 512],
                            start=(fc == 0), stop=(fc == 7)),
                            deps=[wtA, xntok] + (pd if fc == 0 else []), inc=(fc == 7))
                    self.pwrote(self.ps_b, mm2, (j,))
                    self.B(xin).read(mm2)
                    self.B(bufA).read(mm2)
                    vcola = self.b1p_sb[:, l * 44 + co:l * 44 + co + 1]
                    zt = p.op("dve", lambda e, co=co, j=j, vcola=vcola: e.scalar_tensor_tensor(
                        z[:, co * TB + j * 512:co * TB + j * 512 + 512],
                        self.ps_b[:, j * 512:(j + 1) * 512], vcola,
                        gbuf[:, j * 512:(j + 1) * 512], ALU.add, ALU.mult),
                        deps=[mm2, gl, self.B(self.b1p_sb).ready1] +
                             (self.B(z).wdeps() if (co == NCA - 1 and j == 0) else []))
                    self.pread(self.ps_b, zt, (j,))
                    self.B(gbuf).read(zt)
                    self.B(z).wrote(zt)
            ztok = self.B(z).ready1
            w2state = getattr(self, "_w2state" + sfx, None)
            if w2state is None:
                w2state = {}
                setattr(self, "_w2state" + sfx, w2state)
            def ensure_w2(g):
                for k in (g, g + 1):
                    if k < 2:
                        slot = k % 2
                        if w2state.get(slot) != k:
                            w2state[slot] = k
                            w2state[("tok", slot)] = load_w2(k, slot)
                return w2s[g % 2], w2state[("tok", g % 2)]
            for g in range(2):
                bufW, wt2 = ensure_w2(g)
                for jj in range(4):
                    fo = g * 4 + jj
                    for j in range(2):
                        ysel = self.ia_yr[j % 2]
                        ly = p.dma("qG", ysel[:, :], ysrc.ap()[fo, :, t0 + j * 512:t0 + (j + 1) * 512],
                                   deps=self.B(ysel).wdeps())
                        self.B(ysrc).read(ly)
                        self.B(ysel).wrote(ly)
                        pd = self.pwd(self.ps_b, (j,))
                        for f2 in range(CFG.NC2):
                            mm = p.op("pe", lambda e, f2=f2, jj=jj, j=j, bufW=bufW: e.matmul(
                                self.ps_b[:, j * 512:(j + 1) * 512],
                                bufW[:, f2 * 512 + jj * 128:f2 * 512 + jj * 128 + 128],
                                z[:, f2 * TB + j * 512:f2 * TB + j * 512 + 512],
                                start=(f2 == 0), stop=(f2 == CFG.NC2 - 1)),
                                deps=[wt2, ztok] + (pd if f2 == 0 else []),
                                inc=(f2 == CFG.NC2 - 1))
                        self.B(bufW).read(mm)
                        self.pwrote(self.ps_b, mm, (j,))
                        self.B(z).read(mm)
                        b2col = self.b2_sb[:, l * CFG.CH + fo:l * CFG.CH + fo + 1]
                        rs = p.op("dve", lambda e, j=j, b2col=b2col, ysel=ysel: e.scalar_tensor_tensor(
                            self.ia_ym[:, :], self.ps_b[:, j * 512:(j + 1) * 512], b2col, ysel[:, :],
                            ALU.add, ALU.add),
                            deps=[mm, ly, self.B(self.b2_sb).ready1] + self.B(self.ia_ym).wdeps())
                        self.pread(self.ps_b, rs, (j,))
                        self.B(ysel).read(rs)
                        self.B(self.ia_ym).wrote(rs)
                        d = p.dma("qS", ydst.ap()[fo, :, t0 + j * 512:t0 + (j + 1) * 512],
                                  self.ia_ym[:, :], deps=[rs])
                        self.B(self.ia_ym).read(d)
                        self.B(ydst).wrote_add(d)

    def ph_final(self, ysrc):
        p, S = self.p, self.S
        yfb = self.sb("fin_y", [128, 1024], F32)
        gtm = self.sb("fin_g", [128, 1024], F32)
        stm = self.sb("fin_s", [128, 1024], F32)
        xtm = self.sb("fin_x", [128, 1024], F32)
        otm = self.sb("fin_o", [128, 1024], F32)
        for tt in range(S // 128):
            yd = self.B(yfb).wdeps()
            lys = []
            for cc in range(8):
                lys.append(p.dma("qL", yfb[:, cc * 128:(cc + 1) * 128],
                                 ysrc.ap()[cc, :, tt * 128:(tt + 1) * 128], deps=yd))
                self.B(ysrc).read(lys[-1])
            self.B(yfb).wrote(lys[-1])
            pd = self.pwd(self.ps_a)
            for k in range(8):
                tk = p.op("pe", lambda e, k=k: e.transpose(self.ps_a[:, k * 128:(k + 1) * 128],
                                                           yfb[:, k * 128:(k + 1) * 128], self.c_id32[:, :]),
                          deps=[lys[-1]] + (pd if k == 0 else []), inc=(k == 7))
            self.pwrote(self.ps_a, tk)
            self.B(yfb).read(tk)
            sl = p.op("act", lambda e: e.activation(gtm[:, :], self.ps_a[:, :], AF.Silu),
                      deps=[tk] + self.B(gtm).wdeps())
            self.pread(self.ps_a, sl)
            self.B(gtm).wrote(sl)
            ls = p.dma("qL", stm[:, :], self.s_tm.ap()[tt],
                       deps=self.B(stm).wdeps() + self.B(self.s_tm).rdeps())
            self.B(self.s_tm).read(ls)
            self.B(stm).wrote(ls)
            lxx = p.dma("qL", xtm[:, :], self.x_in.ap()[tt * 128:(tt + 1) * 128, :],
                        deps=self.B(xtm).wdeps())
            self.B(xtm).wrote(lxx)
            m1 = p.op("dve", lambda e: e.tensor_mul(otm[:, :], gtm[:, :], stm[:, :]),
                      deps=[sl, ls] + self.B(otm).wdeps())
            self.B(gtm).read(m1)
            self.B(stm).read(m1)
            m2 = p.op("dve", lambda e: e.tensor_add(otm[:, :], otm[:, :], xtm[:, :]), deps=[m1, lxx])
            self.B(xtm).read(m2)
            self.B(otm).wrote(m2)
            d = p.dma("qS", self.out_t.ap()[tt * 128:(tt + 1) * 128, :], otm[:, :], deps=[m2])
            self.B(otm).read(d)


def build_nc():
    nc = bass.Bass("TRN2", target_bir_lowering=False)
    from contextlib import ExitStack
    es = ExitStack()
    with es:
        p = Prog(nc)
        p.make_sems(es)
        k = Kern(nc, es, p)
        k.build()
        p.finalize_block(es)
    return nc


def make_inputs_common(ins):
    consts = host_constants(CFG.S)
    DI, F1H = CFG.DI, CFG.F1H
    b1 = np.asarray(ins["L0_ffn_b1"], dtype=np.float32)
    b1p = np.zeros((CFG.DEPTH, 2 * F1H), dtype=np.float32)
    b1p[:, 0:DI] = b1[:, 0:DI]
    b1p[:, F1H:F1H + DI] = b1[:, DI:2 * DI]
    f32 = lambda a: np.ascontiguousarray(np.asarray(a, dtype=np.float32))
    common = {
        "wq_in": f32(ins["L0_attn_Wq"]), "wk_in": f32(ins["L0_attn_Wk"]),
        "wv_in": f32(ins["L0_attn_Wv"]), "woo_in": f32(ins["L0_attn_Wo"]),
        "gamma_bc": f32(np.broadcast_to(np.asarray(ins["L0_attn_gamma"]), (128, CFG.D))),
        "beta_bc": f32(np.broadcast_to(np.asarray(ins["L0_attn_beta"]), (128, CFG.D))),
        "wqkv_in": f32(ins["L0_ffn_Wqkv"]), "qs_in": f32(ins["L0_ffn_q_scale"]),
        "ks_in": f32(ins["L0_ffn_k_scale"]), "wo_in": f32(ins["L0_ffn_Wo"]),
        "lng_in": f32(ins["L0_ffn_ln_g"]), "lnb_in": f32(ins["L0_ffn_ln_b"]),
        "w1_in": f32(ins["L0_ffn_W1"]), "b1p_in": b1p,
        "w2_in": f32(ins["L0_ffn_W2"]), "b2_in": f32(ins["L0_ffn_b2"]),
    }
    common.update(consts)
    return common


def kernel(**inputs):
    common = make_inputs_common(inputs)
    x = np.asarray(inputs["L0_attn_h"], dtype=np.float32)
    nc = build_nc()
    in_maps = []
    for b in range(8):
        m = dict(common)
        m["x_in"] = np.ascontiguousarray(x[b])
        in_maps.append(m)
    res = run_bass_kernel_spmd(nc, in_maps, core_ids=list(range(8)))
    return np.stack([r["out_t"] for r in res.results], axis=0)


# version 14 (final)
# speedup vs baseline: 1.2052x; 1.2052x over previous
import numpy as np
import ml_dtypes
import concourse.bass as bass
import concourse.mybir as mybir
from concourse.bass_utils import run_bass_kernel_spmd

F32 = mybir.dt.float32
BF16 = mybir.dt.bfloat16
AF = mybir.ActivationFunctionType
ALU = mybir.AluOpType
AX = mybir.AxisListType


class CFG:
    S = 4096
    D = 1024
    CH = 8
    H = 16
    DH = 64
    DEPTH = 6
    WIN = 256
    TB = 1024
    QB = 1024
    DI = 2730
    F1H = 2816
    NCA = 22
    NC2 = 22
    SAMPLE = 16
    LN_EPS = 1e-5


ENG2BLOCK = {"pe": "tensor", "act": "scalar", "dve": "vector", "pool": "gpsimd", "sync": "sync"}
Q2ENG = {"qL": "sync", "qS": "act", "qG": "pool", "qW": "sync", "qX": "pool", "qV": "act", "qF": "sync"}
STREAMS = ("pe", "act", "dve", "pool", "sync")
ALLSEMS = ("pe", "act", "dve", "pool", "qL", "qS", "qG")


class Tok:
    __slots__ = ("sem", "val")
    def __init__(self, sem, val):
        self.sem = sem
        self.val = val
    def __repr__(self):
        return f"{self.sem}>={self.val}"


class Buf:
    __slots__ = ("ready", "readers", "name")
    def __init__(self, name=""):
        self.ready = {}
        self.readers = []
        self.name = name
    def rdeps(self):
        return [Tok(s, v) for s, v in self.ready.items()]

    @property
    def ready1(self):
        if not self.ready:
            return None
        assert len(self.ready) == 1, f"multi-ready buf {self.name}"
        ((s, v),) = self.ready.items()
        return Tok(s, v)
    def wdeps(self):
        return list(self.readers) + self.rdeps()
    def wrote(self, tok):
        if tok is not None:
            self.ready = {tok.sem: tok.val}
            self.readers = []
    def wrote_add(self, tok):
        if tok is not None:
            if self.ready.get(tok.sem, -1) < tok.val:
                self.ready[tok.sem] = tok.val
    def read(self, tok):
        if tok is not None:
            self.readers.append(tok)
            if len(self.readers) > 24:
                self.readers = self.readers[-24:]


class Prog:
    def __init__(self, nc):
        self.nc = nc
        self.ops = {e: [] for e in STREAMS}
        self.cnt = {s: 0 for s in ("pe", "act", "dve", "pool")}
        self.hwm = {e: {} for e in STREAMS}
        self.sems = {}
        self.nops = 0

    def make_sems(self, es):
        self._es = es
        for k in ("pe", "act", "dve", "pool"):
            self.sems[k] = es.enter_context(self.nc.semaphore("sem_" + k))

    def _dma_sem(self, key):
        if key not in self.sems:
            self.sems[key] = self._es.enter_context(self.nc.semaphore("sem_" + key))
            self.cnt[key] = 0
        return self.sems[key]

    def _prep(self, stream, deps, allow_embed=True):
        hw = self.hwm[stream]
        best = {}
        for t in deps:
            if t is None:
                continue
            if hw.get(t.sem, -1) >= t.val:
                continue
            if best.get(t.sem, -1) < t.val:
                best[t.sem] = t.val
        todo = [Tok(s, v) for s, v in best.items()]
        for t in todo:
            hw[t.sem] = t.val
        emb = todo.pop() if (todo and allow_embed) else None
        return todo, emb

    def op(self, stream, fn, deps=(), inc=True):
        pre, emb = self._prep(stream, deps)
        tok = None
        if inc:
            self.cnt[stream] += 1
            tok = Tok(stream, self.cnt[stream])
        sems = self.sems
        def closure(eng, fn=fn, pre=pre, emb=emb, tok=tok, stream=stream):
            for t in pre:
                eng.wait_ge(sems[t.sem], t.val)
            ins = fn(eng)
            if emb is not None:
                ins._wait_ge(sems[emb.sem], emb.val)
            if tok is not None:
                ins.then_inc(sems[stream], 1)
            return ins
        self.ops[stream].append(closure)
        self.nops += 1
        return tok

    def tok(self, stream):
        return Tok(stream, self.cnt[stream])

    def wait(self, stream, deps):
        pre, _ = self._prep(stream, deps, allow_embed=False)
        if not pre:
            return
        sems = self.sems
        def closure(eng, pre=pre):
            for t in pre:
                eng.wait_ge(sems[t.sem], t.val)
        self.ops[stream].append(closure)

    def dma(self, q, out, in_, deps=(), **kw):
        import re as _re
        stream = Q2ENG[q]
        side = out.tensor
        if type(side).__name__.startswith("DRam"):
            side = in_.tensor
        key = "d_" + q + "_" + _re.sub(r"_\d+$", "", side.name)
        self._dma_sem(key)
        pre, _ = self._prep(stream, deps, allow_embed=False)
        self.cnt[key] += 16
        tok = Tok(key, self.cnt[key])
        sems = self.sems
        def closure(eng, out=out, in_=in_, pre=pre, key=key, kw=kw):
            for t in pre:
                eng.wait_ge(sems[t.sem], t.val)
            ins = eng.dma_start(out=out, in_=in_, **kw)
            ins.then_inc(sems[key], 16)
            return ins
        self.ops[stream].append(closure)
        self.nops += 1
        return tok

    def barrier(self):
        toks = [Tok(s, v) for s, v in self.cnt.items() if v > 0]
        for e in STREAMS:
            self.wait(e, toks)

    def finalize_block(self, es):
        block = es.enter_context(self.nc.Block())
        for e, attr in ENG2BLOCK.items():
            ops = self.ops[e]
            def run(eng, ops=ops):
                for f in ops:
                    f(eng)
            getattr(block, attr)(run)


def host_constants(S):
    D, DH, WIN = CFG.D, CFG.DH, CFG.WIN
    c = {}
    c["ident32"] = np.eye(128, dtype=np.float32)
    c["identbf"] = np.eye(128, dtype=ml_dtypes.bfloat16)
    c["ones_1x128"] = np.ones((1, 128), dtype=ml_dtypes.bfloat16)
    c["ones_128x1"] = np.ones((128, 1), dtype=ml_dtypes.bfloat16)
    half = np.arange(0, DH, 2, dtype=np.float64)
    inv_freq = 1.0 / 10000.0 ** (half / DH)
    t = np.arange(S, dtype=np.float64)
    freqs = t[:, None] * inv_freq[None, :]
    freqs = np.concatenate([freqs, freqs], -1)
    base_scale = (half + 0.4 * DH) / (1.4 * DH)
    power = (t - S // 2) / (WIN // 2)
    scale = base_scale[None, :] ** power[:, None]
    scale = np.concatenate([scale, scale], -1)
    cos, sin = np.cos(freqs), np.sin(freqs)
    def fm(a):
        return np.ascontiguousarray(np.tile(a.T, (2, 1))).astype(ml_dtypes.bfloat16)
    c["tab_cs"] = fm(cos * scale)
    c["tab_ss"] = fm(sin * scale)
    c["tab_ci"] = fm(cos / scale)
    c["tab_si"] = fm(sin / scale)
    R = np.zeros((128, 128), dtype=np.float32)
    for base in (0, 64):
        for f in range(32):
            R[base + f + 32, base + f] = -1.0
        for f in range(32, 64):
            R[base + f - 32, base + f] = 1.0
    c["rperm"] = R.astype(ml_dtypes.bfloat16)
    esel = np.zeros((CFG.CH, 128, 16), dtype=np.float32)
    for cc in range(CFG.CH):
        esel[cc, 0:64, 2 * cc] = 1.0
        esel[cc, 64:128, 2 * cc + 1] = 1.0
    c["esel"] = esel.astype(ml_dtypes.bfloat16)
    epr = np.zeros((CFG.CH, 16, 128), dtype=np.float32)
    for cc in range(CFG.CH):
        epr[cc, 2 * cc, 0:64] = 1.0
        epr[cc, 2 * cc + 1, 64:128] = 1.0
    c["eprime"] = epr.astype(ml_dtypes.bfloat16)
    m = np.zeros((128, 1024), dtype=np.float32)
    for kt in range(4):
        for pp in range(128):
            qw = np.arange(256)
            delta = qw + 256 - kt * 128 - pp
            ok = (delta >= 0) & (delta <= 256)
            m[pp, kt * 256:(kt + 1) * 256] = ok.astype(np.float32)
    c["maskT"] = m.astype(ml_dtypes.bfloat16)
    m0 = m.copy()
    m0[:, 0:512] = 0.0
    c["maskW0"] = m0.astype(ml_dtypes.bfloat16)
    return c


class Kern:
    def __init__(self, nc, es, p):
        self.nc = nc
        self.es = es
        self.p = p
        self.bufs = {}
        self.S = CFG.S
        self._scopes = []

    def sb(self, name, shape, dtype):
        es = self._scopes[-1] if self._scopes else self.es
        t = es.enter_context(self.nc.sbuf_tensor(name, shape, dtype))
        self.bufs[name] = Buf(name)
        return t

    def psum(self, name, shape, dtype=F32):
        t = self.es.enter_context(self.nc.psum_tensor(name, shape, dtype))
        self.bufs[name] = Buf(name)
        self.bufs[name + ":0"] = Buf(name + ":0")
        self.bufs[name + ":1"] = Buf(name + ":1")
        return t

    def pb(self, ps, r):
        return self.bufs[ps.name + f":{r}"]

    def pwd(self, ps, regs=(0, 1)):
        d = []
        for r in regs:
            d += self.pb(ps, r).wdeps()
        return d

    def prd(self, ps, regs=(0, 1)):
        d = []
        for r in regs:
            d += self.pb(ps, r).rdeps()
        return d

    def pwrote(self, ps, tok, regs=(0, 1)):
        for r in regs:
            self.pb(ps, r).wrote(tok)

    def pwrote0(self, ps, tok):
        self.pwrote(ps, tok, (0,))

    def pread0(self, ps, tok):
        self.pread(ps, tok, (0,))

    def pread(self, ps, tok, regs=(0, 1)):
        for r in regs:
            self.pb(ps, r).read(tok)

    def dram(self, name, shape, dtype, kind="Internal"):
        t = self.nc.dram_tensor(name, shape, dtype, kind=kind)
        self.bufs[name] = Buf(name)
        return t

    def B(self, x):
        return self.bufs[x if isinstance(x, str) else x.name]

    def LB(self, t, l):
        key = t.name + f"@{l}"
        if key not in self.bufs:
            self.bufs[key] = Buf(key)
        return self.bufs[key]

    from contextlib import contextmanager
    @contextmanager
    def scope(self):
        from contextlib import ExitStack
        es = ExitStack()
        self._scopes.append(es)
        try:
            with es:
                yield
        finally:
            self._scopes.pop()
            self.p.barrier()

    def build(self):
        p, S = self.p, self.S
        self.x_in = self.dram("x_in", [S, CFG.D], F32, kind="ExternalInput")
        self.wq_in = self.dram("wq_in", [CFG.D, CFG.D], F32, kind="ExternalInput")
        self.wk_in = self.dram("wk_in", [CFG.D, CFG.D], F32, kind="ExternalInput")
        self.wv_in = self.dram("wv_in", [CFG.D, CFG.D], F32, kind="ExternalInput")
        self.woo_in = self.dram("woo_in", [CFG.D, CFG.D], F32, kind="ExternalInput")
        self.gamma_in = self.dram("gamma_bc", [128, CFG.D], F32, kind="ExternalInput")
        self.beta_in = self.dram("beta_bc", [128, CFG.D], F32, kind="ExternalInput")
        self.wqkv_in = self.dram("wqkv_in", [CFG.DEPTH, CFG.D, 3 * CFG.D], F32, kind="ExternalInput")
        self.qs_in = self.dram("qs_in", [CFG.DEPTH, CFG.DH], F32, kind="ExternalInput")
        self.ks_in = self.dram("ks_in", [CFG.DEPTH, CFG.DH], F32, kind="ExternalInput")
        self.wo_in = self.dram("wo_in", [CFG.DEPTH, CFG.D, CFG.D], F32, kind="ExternalInput")
        self.lng_in = self.dram("lng_in", [CFG.DEPTH, CFG.D], F32, kind="ExternalInput")
        self.lnb_in = self.dram("lnb_in", [CFG.DEPTH, CFG.D], F32, kind="ExternalInput")
        self.w1_in = self.dram("w1_in", [CFG.DEPTH, CFG.D, 2 * CFG.DI], F32, kind="ExternalInput")
        self.b1p_in = self.dram("b1p_in", [CFG.DEPTH, 2 * CFG.F1H], F32, kind="ExternalInput")
        self.w2_in = self.dram("w2_in", [CFG.DEPTH, CFG.DI, CFG.D], F32, kind="ExternalInput")
        self.b2_in = self.dram("b2_in", [CFG.DEPTH, CFG.D], F32, kind="ExternalInput")
        for cname, shape, dt in [
            ("ident32", [128, 128], F32), ("identbf", [128, 128], BF16),
            ("ones_1x128", [1, 128], BF16), ("ones_128x1", [128, 1], BF16),
            ("tab_cs", [128, S], BF16), ("tab_ss", [128, S], BF16),
            ("tab_ci", [128, S], BF16), ("tab_si", [128, S], BF16),
            ("rperm", [128, 128], BF16), ("esel", [CFG.CH, 128, 16], BF16),
            ("eprime", [CFG.CH, 16, 128], BF16),
            ("maskT", [128, 1024], BF16), ("maskW0", [128, 1024], BF16),
        ]:
            setattr(self, "d_" + cname, self.dram(cname, shape, dt, kind="ExternalInput"))
        self.out_t = self.dram("out_t", [S, CFG.D], F32, kind="ExternalOutput")

        dr = self.dram
        CHT = S // 128
        self.xn_tm = dr("xn_tm", [CHT, 128, CFG.D], F32)
        self.xn_bf = dr("xn_bf", [CFG.CH, S // 512, 128, 512], BF16)
        self.y0 = dr("y0", [CFG.CH, 128, S], F32)
        self.yA = dr("yA", [CFG.CH, 128, S], F32)
        self.yB = dr("yB", [CFG.CH, 128, S], F32)
        self.yM = dr("yM", [CFG.CH, 128, S], F32)
        self.q_d = dr("q_d", [CFG.H, S // 512, 64, 512], BF16)
        self.k_d = dr("k_d", [CFG.H, S // 512, 64, 512], BF16)
        self.v_d = dr("v_d", [CHT, 128, 1040], BF16)
        self.o2_d = dr("o2_d", [CFG.CH, S // 512, 128, 512], BF16)
        self.s_tm = dr("s_tm", [CHT, 128, CFG.D], F32)
        self.wq_bf = dr("wq_bf", [CFG.CH, 2, 128, 512], BF16)
        self.wk_bf = dr("wk_bf", [CFG.CH, 2, 128, 512], BF16)
        self.wv_bf = dr("wv_bf", [CFG.CH, 2, 128, 512], BF16)
        self.woo_bf = dr("woo_bf", [CFG.CH, 2, 128, 512], BF16)
        self.wqkv_bf = dr("wqkv_bf", [CFG.DEPTH, CFG.CH, 6, 128, 512], BF16)
        self.wo_bf = dr("wo_bf", [CFG.DEPTH, CFG.CH, 2, 128, 512], BF16)
        self.w1_bf = dr("w1_bf", [CFG.DEPTH, CFG.CH, 11, 128, 512], BF16)
        self.w2_bf = dr("w2_bf", [CFG.DEPTH, CFG.NC2, 2, 128, 512], BF16)

        sb = self.sb
        self.c_id32 = sb("c_id32", [128, 128], F32)
        self.c_idbf = sb("c_idbf", [128, 128], BF16)
        self.c_ones1x = sb("c_ones1x", [1, 128], BF16)
        self.c_ones128 = sb("c_ones128", [128, 1], BF16)
        self.c_ones65 = sb("c_ones65", [65, 128], BF16)
        self.c_rperm = sb("c_rperm", [128, 128], BF16)
        self.c_esel = sb("c_esel", [128, CFG.CH * 16], BF16)
        self.c_eprime = sb("c_eprime", [16, CFG.CH * 128], BF16)
        self.c_maskT = sb("c_maskT", [128, 1024], BF16)
        self.c_maskW0 = sb("c_maskW0", [128, 1024], BF16)
        self.lng_sb = sb("lng_sb", [128, CFG.DEPTH * CFG.CH], F32)
        self.negc_sb = sb("negc_sb", [128, CFG.DEPTH], F32)
        self.qs8_sb = sb("qs8_sb", [128, CFG.DEPTH], F32)
        self.qs8p_sb = sb("qs8p_sb", [128, CFG.DEPTH], F32)
        self.ks_sb = sb("ks_sb", [128, CFG.DEPTH], F32)
        self.ksp_sb = sb("ksp_sb", [128, CFG.DEPTH], F32)
        self.b2_sb = sb("b2_sb", [128, CFG.DEPTH * CFG.CH], F32)
        self.b1p_sb = sb("b1p_sb", [128, CFG.DEPTH * 44], F32)
        self.wch = [sb("wch0", [128, 1536], F32), sb("wch1", [128, 1536], F32)]
        self.wch_par = 0
        self.w16 = sb("w16", [128, 3072], BF16)
        self.ia_xi = sb("ia_xi", [128, 8 * CFG.TB], BF16)
        self.ia_yt = [sb("ia_yt0", [128, CFG.TB + 1], F32), sb("ia_yt1", [128, CFG.TB + 1], F32)]
        self.ia_ym = sb("ia_ym", [128, 512], F32)
        self.ia_yr = [sb("ia_yr0", [128, 512], F32), sb("ia_yr1", [128, 512], F32)]
        self.fs_rows = sb("fs_rows", [65, S], BF16)
        self.fs_tmp = sb("fs_tmp", [1, 1024], F32)
        self.fs_x2 = sb("fs_x2", [128, 512], BF16)

        self.ps_a = self.psum("ps_a", [128, 1024], F32)
        self.ps_b = self.psum("ps_b", [128, 1024], F32)
        self.ps_c = self.psum("ps_c", [128, 1024], F32)
        self.ps_m = self.psum("ps_m", [128, 512], F32)
        self.ps_t = self.psum("ps_t", [128, 1024], BF16)

        import os
        phlim = int(os.environ.get("KPHASES", "999"))
        phn = [0]
        def phase_on():
            phn[0] += 1
            return phn[0] <= phlim
        if phase_on():
            self.ph_init()
        if phase_on():
            self.ph_wcast_outer()
        if phase_on():
            with self.scope():
                self.ph_xn()
        if phase_on():
            for th in self.wcast_layer_thunks(0):
                th()
        if phase_on():
            with self.scope():
                self.ph_outer_qkv()
        if phase_on():
            with self.scope():
                self.ph_outer_attn()
        if phase_on():
            with self.scope():
                self.ph_outer_wos()
        ybufs = {0: self.y0, 1: self.yA, 2: self.yB}
        cur = 0
        for l in range(CFG.DEPTH):
            trickle = []
            if l + 1 < CFG.DEPTH and phase_on():
                trickle = self.wcast_layer_thunks(l + 1)
            ysrc = ybufs[cur]
            ydst = self.yA if ysrc is not self.yA else self.yB
            if phase_on():
                with self.scope():
                    self.ph_inner_attn(l, ysrc, self.yM, trickle)
            elif trickle:
                for th in trickle:
                    th()
            if phase_on():
                with self.scope():
                    self.ph_inner_ffn_main(l, self.yM, ydst)
            cur = 1 if ydst is self.yA else 2
        if phase_on():
            with self.scope():
                self.ph_final(ybufs[cur])
        p.wait("sync", [Tok(s, v) for s, v in p.cnt.items() if v > 0])

    def ph_init(self):
        p = self.p
        for sbuf, d in [(self.c_id32, self.d_ident32), (self.c_idbf, self.d_identbf),
                        (self.c_ones1x, self.d_ones_1x128), (self.c_ones128, self.d_ones_128x1),
                        (self.c_rperm, self.d_rperm), (self.c_maskT, self.d_maskT),
                        (self.c_maskW0, self.d_maskW0)]:
            self.B(sbuf).wrote(p.dma("qL", sbuf[:, :], d.ap(), deps=[]))
        o65 = p.op("dve", lambda e: e.memset(self.c_ones65[:, :], 1.0), deps=[])
        self.B(self.c_ones65).wrote(o65)
        for cc in range(CFG.CH):
            t1 = p.dma("qL", self.c_esel[:, cc * 16:(cc + 1) * 16], self.d_esel.ap()[cc], deps=[])
            t2 = p.dma("qL", self.c_eprime[0:16, cc * 128:(cc + 1) * 128], self.d_eprime.ap()[cc], deps=[])
        self.B(self.c_esel).wrote(t1)
        self.B(self.c_eprime).wrote(t2)
        for l in range(CFG.DEPTH):
            for cc in range(CFG.CH):
                tok = p.dma("qL", self.b2_sb[:, l * CFG.CH + cc:l * CFG.CH + cc + 1],
                            self.b2_in.ap()[l, cc * 128:(cc + 1) * 128].rearrange("(f o) -> f o", o=1),
                            deps=[], allow_slow_non_contiguous=True)
        self.B(self.b2_sb).wrote(tok)
        lnb32 = self.sb("lnb32", [128, CFG.DEPTH * CFG.CH], F32)
        for l in range(CFG.DEPTH):
            for cc in range(CFG.CH):
                tok = p.dma("qL", lnb32[:, l * CFG.CH + cc:l * CFG.CH + cc + 1],
                            self.lnb_in.ap()[l, cc * 128:(cc + 1) * 128].rearrange("(f o) -> f o", o=1),
                            deps=[], allow_slow_non_contiguous=True)
        self.B(lnb32).wrote(tok)
        self.lnb_sb = lnb32
        for l in range(CFG.DEPTH):
            for cc in range(CFG.CH):
                tok = p.dma("qL", self.lng_sb[:, l * CFG.CH + cc:l * CFG.CH + cc + 1],
                            self.lng_in.ap()[l, cc * 128:(cc + 1) * 128].rearrange("(f o) -> f o", o=1),
                            deps=[], allow_slow_non_contiguous=True)
        self.B(self.lng_sb).wrote(tok)
        for l in range(CFG.DEPTH):
            for cc in range(44):
                tok = p.dma("qL", self.b1p_sb[:, l * 44 + cc:l * 44 + cc + 1],
                            self.b1p_in.ap()[l, cc * 128:(cc + 1) * 128].rearrange("(f o) -> f o", o=1),
                            deps=[], allow_slow_non_contiguous=True)
        self.B(self.b1p_sb).wrote(tok)
        lastq = {}
        for l in range(CFG.DEPTH):
            for (dst, src, perm) in ((self.qs8_sb, self.qs_in, False), (self.qs8p_sb, self.qs_in, True),
                                     (self.ks_sb, self.ks_in, False), (self.ksp_sb, self.ks_in, True)):
                for hb in range(2):
                    base = hb * 64
                    if not perm:
                        tok = p.dma("qL", dst[base:base + 64, l:l + 1],
                                    src.ap()[l, 0:64].rearrange("(f o) -> f o", o=1), deps=[],
                                    allow_slow_non_contiguous=True)
                    else:
                        tok = p.dma("qL", dst[base:base + 32, l:l + 1],
                                    src.ap()[l, 32:64].rearrange("(f o) -> f o", o=1), deps=[],
                                    allow_slow_non_contiguous=True)
                        tok = p.dma("qL", dst[base + 32:base + 64, l:l + 1],
                                    src.ap()[l, 0:32].rearrange("(f o) -> f o", o=1), deps=[],
                                    allow_slow_non_contiguous=True)
                    lastq[dst.name] = tok
        t1 = p.op("dve", lambda e: e.tensor_scalar_mul(self.qs8_sb[:, :], self.qs8_sb[:, :], 8.0),
                  deps=[lastq[self.qs8_sb.name]])
        t2 = p.op("dve", lambda e: e.tensor_scalar_mul(self.qs8p_sb[:, :], self.qs8p_sb[:, :], 8.0),
                  deps=[t1, lastq[self.qs8p_sb.name]])
        self.B(self.qs8_sb).wrote(t1)
        self.B(self.qs8p_sb).wrote(t2)
        self.B(self.ks_sb).wrote(lastq[self.ks_sb.name])
        self.B(self.ksp_sb).wrote(lastq[self.ksp_sb.name])
        tiny = self.sb("tiny", [1, 256], F32)
        tinyb = self.sb("tinyb", [1, 8], BF16)
        for l in range(CFG.DEPTH):
            ta = p.dma("qL", tiny[0:1, 0:64], self.qs_in.ap()[l].rearrange("(o f) -> o f", o=1),
                       deps=self.B(tiny).wdeps())
            tb = p.dma("qL", tiny[0:1, 64:128], self.ks_in.ap()[l].rearrange("(o f) -> o f", o=1), deps=[])
            u1 = p.op("dve", lambda e: e.reduce_max(tiny[0:1, 128:129], tiny[0:1, 0:64], axis=AX.X,
                                                    apply_absolute_value=True), deps=[ta, tb])
            u2 = p.op("dve", lambda e: e.reduce_max(tiny[0:1, 129:130], tiny[0:1, 64:128], axis=AX.X,
                                                    apply_absolute_value=True), deps=[u1])
            u3 = p.op("dve", lambda e: e.tensor_mul(tiny[0:1, 130:131], tiny[0:1, 128:129],
                                                    tiny[0:1, 129:130]), deps=[u2])
            u4 = p.op("dve", lambda e: e.tensor_scalar_mul(tiny[0:1, 131:132], tiny[0:1, 130:131], -8.0),
                      deps=[u3])
            u5 = p.op("dve", lambda e, l=l: e.tensor_copy(tinyb[0:1, l:l + 1], tiny[0:1, 131:132]), deps=[u4])
            self.B(tiny).wrote(u5)
            mm = p.op("pe", lambda e, l=l: e.matmul(self.ps_m[:, 0:1], self.c_ones1x[0:1, :],
                                                    tinyb[0:1, l:l + 1], start=True, stop=True),
                      deps=[u5, self.B(self.c_ones1x).ready1] + self.pwd(self.ps_m, (0,)))
            self.pwrote0(self.ps_m, mm)
            ev = p.op("dve", lambda e, l=l: e.tensor_copy(self.negc_sb[:, l:l + 1], self.ps_m[:, 0:1]),
                      deps=[mm])
            self.pread0(self.ps_m, ev)
        self.B(self.negc_sb).wrote(ev)

    def _wcast_piece(self, src_ap, F, dst_aps, npart=128, pads=(), srcw=None):
        p = self.p
        w16 = self.w16
        b16 = self.B(w16)
        SW = srcw or F
        CW = 1536
        last = None
        for c0 in range(0, F, CW):
            c1 = min(F, c0 + CW)
            buf = self.wch[self.wch_par % 2]
            self.wch_par += 1
            bb = self.B(buf)
            deps = bb.wdeps()
            if npart < 128 or c1 > SW:
                ms = p.op("pool", lambda e, buf=buf, c0=c0, c1=c1: e.memset(buf[:, 0:c1 - c0], 0.0),
                          deps=deps)
                deps = [ms]
            s1 = min(SW, c1)
            ld = p.dma("qW", buf[0:npart, 0:s1 - c0], src_ap[:, c0:s1], deps=deps)
            bb.wrote(ld)
            ct = p.op("pool", lambda e, buf=buf, c0=c0, c1=c1: e.tensor_copy(
                w16[:, c0:c1], buf[:, 0:c1 - c0]), deps=[ld] + (b16.wdeps() if c0 == 0 else []))
            bb.read(ct)
            last = ct
        b16.wrote(last)
        for (dap, c0, wd) in dst_aps:
            sap = w16[:, c0:c0 + wd]
            if wd % 512 == 0 and wd > 512:
                sap = sap.rearrange("p (g j) -> p g j", j=512)
            st = p.dma("qX", dap, sap, deps=[last])
            b16.read(st)
        return st

    def ph_wcast_outer(self):
        p = self.p
        D = CFG.D
        for (src, dst) in ((self.wq_in, self.wq_bf), (self.wk_in, self.wk_bf),
                           (self.wv_in, self.wv_bf), (self.woo_in, self.woo_bf)):
            for fc in range(CFG.CH):
                dstv = dst.ap()[fc].rearrange("g p j -> p g j")
                st = self._wcast_piece(src.ap()[fc * 128:(fc + 1) * 128, :], D, [(dstv, 0, D)])
            self.B(dst).wrote_add(st)

    def wcast_layer_thunks(self, l):
        D, DI, F1H = CFG.D, CFG.DI, CFG.F1H
        th = []
        def piece(dst, fn):
            def run(fn=fn, dst=dst):
                st = fn()
                self.LB(dst, l).wrote_add(st)
            th.append(run)
        for fc in range(CFG.CH):
            piece(self.wqkv_bf, lambda fc=fc: self._wcast_piece(
                self.wqkv_in.ap()[l, fc * 128:(fc + 1) * 128, :], 3 * D,
                [(self.wqkv_bf.ap()[l, fc].rearrange("g p j -> p g j"), 0, 3 * D)]))
        for fc in range(CFG.CH):
            piece(self.wo_bf, lambda fc=fc: self._wcast_piece(
                self.wo_in.ap()[l, fc * 128:(fc + 1) * 128, :], D,
                [(self.wo_bf.ap()[l, fc].rearrange("g p j -> p g j"), 0, D)]))
        for fc in range(CFG.CH):
            for half in range(2):
                def w1fn(fc=fc, half=half):
                    w1v = self.w1_bf.ap()[l, fc].rearrange("g p j -> p g j")
                    if half == 0:
                        dst_aps = [(w1v[:, 0:5, :], 0, 2560), (w1v[:, 5, 0:256], 2560, 256)]
                    else:
                        dst_aps = [(w1v[:, 5, 256:512], 0, 256), (w1v[:, 6:11, :], 256, 2560)]
                    return self._wcast_piece(
                        self.w1_in.ap()[l, fc * 128:(fc + 1) * 128, half * DI:(half + 1) * DI],
                        F1H, dst_aps, pads=(1,), srcw=DI)
                piece(self.w1_bf, w1fn)
        for fc in range(CFG.NC2):
            r0 = fc * 128
            r1 = min(DI, r0 + 128)
            piece(self.w2_bf, lambda fc=fc, r0=r0, r1=r1: self._wcast_piece(
                self.w2_in.ap()[l, r0:r1, :], D,
                [(self.w2_bf.ap()[l, fc].rearrange("g p j -> p g j"), 0, D)], npart=r1 - r0))
        return th


    def _ln_rows(self, stt, z2s, src, dep):
        p = self.p
        s1 = p.op("dve", lambda e: e.reduce_sum(stt[:, 0:1], src[:, :], axis=AX.X),
                  deps=[dep] + self.B(stt).wdeps())
        sq = p.op("act", lambda e: e.activation(z2s[:, :], src[:, :], AF.Square),
                  deps=[dep] + self.B(z2s).wdeps())
        self.B(z2s).wrote(sq)
        s2 = p.op("dve", lambda e: e.reduce_sum(stt[:, 1:2], z2s[:, :], axis=AX.X), deps=[sq, s1])
        self.B(z2s).read(s2)
        mu = p.op("dve", lambda e: e.tensor_scalar_mul(stt[:, 2:3], stt[:, 0:1], 1.0 / CFG.D), deps=[s2])
        m2 = p.op("dve", lambda e: e.tensor_mul(stt[:, 3:4], stt[:, 2:3], stt[:, 2:3]), deps=[mu])
        var = p.op("dve", lambda e: e.scalar_tensor_tensor(stt[:, 4:5], stt[:, 1:2], 1.0 / CFG.D,
                                                           stt[:, 3:4], ALU.mult, ALU.subtract), deps=[m2])
        ve = p.op("dve", lambda e: e.tensor_scalar_add(stt[:, 4:5], stt[:, 4:5], CFG.LN_EPS), deps=[var])
        ln = p.op("act", lambda e: e.activation(stt[:, 5:6], stt[:, 4:5], AF.Ln), deps=[ve])
        rs = p.op("act", lambda e: e.activation(stt[:, 6:7], stt[:, 5:6], AF.Exp, scale=-0.5), deps=[ln])
        nm = p.op("dve", lambda e: e.tensor_mul(stt[:, 7:8], stt[:, 2:3], stt[:, 6:7]), deps=[rs])
        nm2 = p.op("dve", lambda e: e.tensor_scalar_mul(stt[:, 7:8], stt[:, 7:8], -1.0), deps=[nm])
        self.B(stt).wrote(nm2)
        return nm2

    def ph_xn(self):
        p, S = self.p, self.S
        xt = self.sb("xt", [128, 1024], F32)
        z2s = self.sb("xz2", [128, 1024], F32)
        xnt = self.sb("xnt", [128, 1024], F32)
        xnb = self.sb("xnb", [128, 1024], BF16)
        xfm = self.sb("xfm", [128, 1024], F32)
        xfb = self.sb("xfb", [128, 1024], BF16)
        stt = self.sb("xstats", [128, 8], F32)
        for tt in range(S // 128):
            ld = p.dma("qL", xt[:, :], self.x_in.ap()[tt * 128:(tt + 1) * 128, :], deps=self.B(xt).wdeps())
            self.B(xt).wrote(ld)
            st = self._ln_rows(stt, z2s, xt, ld)
            ap = p.op("dve", lambda e: e.tensor_scalar(xnt[:, :], xt[:, :], stt[:, 6:7], stt[:, 7:8],
                                                       ALU.mult, ALU.add), deps=[st, ld] + self.B(xnt).wdeps())
            self.B(stt).read(ap)
            self.B(xnt).wrote(ap)
            self.B(xt).read(ap)
            std = p.dma("qS", self.xn_tm.ap()[tt], xnt[:, :], deps=[ap])
            self.B(xnt).read(std)
            self.B(self.xn_tm).wrote_add(std)
            cb = p.op("dve", lambda e: e.tensor_copy(xnb[:, :], xnt[:, :]), deps=[ap] + self.B(xnb).wdeps())
            self.B(xnb).wrote(cb)
            self.B(xnt).read(cb)
            for k in range(8):
                tk = p.op("pe", lambda e, k=k: e.transpose(self.ps_t[:, k * 128:(k + 1) * 128],
                                                           xnb[:, k * 128:(k + 1) * 128], self.c_idbf[:, :]),
                          deps=[cb, self.B(self.c_idbf).ready1] +
                               (self.pwd(self.ps_t, (0,)) if k == 0 else []), inc=(k == 7))
            self.pwrote0(self.ps_t, tk)
            self.B(xnb).read(tk)
            ev = p.op("dve", lambda e: e.tensor_copy(xfb[:, :], self.ps_t[:, :]),
                      deps=[tk] + self.B(xfb).wdeps())
            self.pread0(self.ps_t, ev)
            self.B(xfb).wrote(ev)
            for cc in range(8):
                dst = self.xn_bf.ap()[cc, tt // 4, :, (tt % 4) * 128:(tt % 4) * 128 + 128]
                dtok = p.dma("qS", dst, xfb[:, cc * 128:(cc + 1) * 128], deps=[ev])
                self.B(xfb).read(dtok)
                self.B(self.xn_bf).wrote_add(dtok)
            for k in range(8):
                tk = p.op("pe", lambda e, k=k: e.transpose(self.ps_a[:, k * 128:(k + 1) * 128],
                                                           xnt[:, k * 128:(k + 1) * 128], self.c_id32[:, :]),
                          deps=[ap, self.B(self.c_id32).ready1] +
                               (self.pwd(self.ps_a) if k == 0 else []), inc=(k == 7))
            self.pwrote(self.ps_a, tk)
            self.B(xnt).read(tk)
            ev2 = p.op("dve", lambda e: e.tensor_copy(xfm[:, :], self.ps_a[:, :]),
                       deps=[tk] + self.B(xfm).wdeps())
            self.pread(self.ps_a, ev2)
            self.B(xfm).wrote(ev2)
            for cc in range(8):
                dtok2 = p.dma("qS", self.y0.ap()[cc, :, tt * 128:(tt + 1) * 128],
                              xfm[:, cc * 128:(cc + 1) * 128], deps=[ev2])
                self.B(xfm).read(dtok2)
                self.B(self.y0).wrote_add(dtok2)


    def ph_outer_qkv(self):
        p, S = self.p, self.S
        xsb = [self.sb("oq_x0", [128, 8 * 512], BF16), self.sb("oq_x1", [128, 8 * 512], BF16)]
        wq_sb = self.sb("oq_wq", [128, 16 * 512], BF16)
        wk_sb = self.sb("oq_wk", [128, 16 * 512], BF16)
        wv_sb = self.sb("oq_wv", [128, 16 * 512], BF16)
        ev = [self.sb(f"oq_ev{i}", [128, 512], BF16) for i in range(4)]
        vtmp = [self.sb("oq_vt0", [128, 1040], BF16), self.sb("oq_vt1", [128, 1040], BF16)]
        for vt in vtmp:
            ms = p.op("dve", lambda e, vt=vt: e.memset(
                vt[:, :].rearrange("p (h j) -> p h j", j=65)[:, :, 64:65], 1.0), deps=[])
            self.B(vt).wrote(ms)
        wtoks = {}
        for (wt, wsb) in ((self.wq_bf, wq_sb), (self.wk_bf, wk_sb), (self.wv_bf, wv_sb)):
            for fc in range(8):
                for g in range(2):
                    tok = p.dma("qV", wsb[:, (g * 8 + fc) * 512:(g * 8 + fc + 1) * 512],
                                wt.ap()[fc, g], deps=[] if (fc == 0 and g == 0) else [])
                    self.B(wt).read(tok)
            self.B(wsb).wrote(tok)
            wtoks[wsb.name] = tok
        for qb in range(S // 512):
            xb = xsb[qb % 2]
            xdeps = self.B(xb).wdeps() + self.B(self.xn_bf).rdeps()
            for cc in range(8):
                xtok = p.dma("qL", xb[:, cc * 512:(cc + 1) * 512], self.xn_bf.ap()[cc, qb], deps=xdeps)
                self.B(self.xn_bf).read(xtok)
            self.B(xb).wrote(xtok)
            for (wsb, dstd, isq) in ((wq_sb, self.q_d, True), (wk_sb, self.k_d, False)):
                wtok = wtoks[wsb.name]
                for co in range(8):
                    g, jj = co // 4, co % 4
                    reg = co % 2
                    pd = self.pwd(self.ps_a, (reg,))
                    for fc in range(8):
                        mm = p.op("pe", lambda e, fc=fc, g=g, jj=jj, reg=reg, wsb=wsb, xb=xb: e.matmul(
                            self.ps_a[:, reg * 512:(reg + 1) * 512],
                            wsb[:, (g * 8 + fc) * 512 + jj * 128:(g * 8 + fc) * 512 + jj * 128 + 128],
                            xb[:, fc * 512:(fc + 1) * 512], start=(fc == 0), stop=(fc == 7)),
                            deps=[wtok, xtok] + (pd if fc == 0 else []), inc=(fc == 7))
                    self.pwrote(self.ps_a, mm, (reg,))
                    self.B(wsb).read(mm)
                    self.B(xb).read(mm)
                    eb = ev[co % 4]
                    if isq:
                        cv = p.op("dve", lambda e, reg=reg, eb=eb: e.tensor_scalar_mul(
                            eb[:, :], self.ps_a[:, reg * 512:(reg + 1) * 512], 0.125),
                            deps=[mm] + self.B(eb).wdeps())
                    else:
                        cv = p.op("dve", lambda e, reg=reg, eb=eb: e.tensor_copy(
                            eb[:, :], self.ps_a[:, reg * 512:(reg + 1) * 512]),
                            deps=[mm] + self.B(eb).wdeps())
                    self.pread(self.ps_a, cv, (reg,))
                    self.B(eb).wrote(cv)
                    d1 = p.dma("qS", dstd.ap()[2 * co, qb], eb[0:64, :], deps=[cv])
                    d2 = p.dma("qS", dstd.ap()[2 * co + 1, qb], eb[64:128, :], deps=[cv])
                    self.B(eb).read(d1)
                    self.B(eb).read(d2)
                    self.B(dstd).wrote_add(d2)
            wtok = wtoks[wv_sb.name]
            for tl in range(4):
                tt = qb * 4 + tl
                vt = vtmp[tl % 2]
                for g in range(2):
                    pd = self.pwd(self.ps_b, (g,))
                    for fc in range(8):
                        mm = p.op("pe", lambda e, fc=fc, g=g, tl=tl, xb=xb: e.matmul(
                            self.ps_b[:, g * 512:(g + 1) * 512],
                            xb[:, fc * 512 + tl * 128:fc * 512 + tl * 128 + 128],
                            wv_sb[:, (g * 8 + fc) * 512:(g * 8 + fc + 1) * 512],
                            start=(fc == 0), stop=(fc == 7)),
                            deps=[wtok, xtok] + (pd if fc == 0 else []), inc=(fc == 7))
                    self.pwrote(self.ps_b, mm, (g,))
                    self.B(wv_sb).read(mm)
                    self.B(xb).read(mm)
                    cv = p.op("dve", lambda e, g=g, vt=vt: e.tensor_copy(
                        vt[:, :].rearrange("p (h j) -> p h j", j=65)[:, g * 8:(g + 1) * 8, 0:64],
                        self.ps_b[:, g * 512:(g + 1) * 512]), deps=[mm] + self.B(vt).wdeps())
                    self.pread(self.ps_b, cv, (g,))
                    self.B(vt).wrote(cv)
                dv = p.dma("qS", self.v_d.ap()[tt], vt[:, :], deps=[cv])
                self.B(vt).read(dv)
                self.B(self.v_d).wrote_add(dv)

    def ph_outer_attn(self):
        p, S = self.p, self.S
        NKT = S // 128
        QB = CFG.QB
        NQB = S // QB
        NS = S // CFG.SAMPLE
        k_sb = self.sb("oa_k", [65, 4 * S], BF16)
        v_sb = self.sb("oa_v", [128, NKT, 260], BF16)
        q_sb = self.sb("oa_q", [65, 4 * QB], BF16)
        ksub = self.sb("oa_ksub", [64, 4 * NS], BF16)
        pt = [self.sb("oa_pt0", [128, QB], BF16), self.sb("oa_pt1", [128, QB], BF16)]
        mneg = self.sb("oa_mneg", [128, QB // 128], F32)
        mnegb = self.sb("oa_mnegb", [QB // 128, 128], BF16)
        rrow = self.sb("oa_rrow", [1, QB], BF16)
        rrow32 = self.sb("oa_rrow32", [1, QB], F32)
        ostage = self.sb("oa_ost", [65, QB], F32)
        bcb = self.sb("oa_bcb", [64, QB], BF16)
        o2 = self.sb("oa_o2", [128, QB], BF16)
        ones_ms = p.op("dve", lambda e: e.memset(k_sb[64:65, :], 1.0), deps=[])
        self.B(k_sb).wrote(ones_ms)
        for hg in range(CFG.H // 4):
            vdeps = self.B(v_sb).wdeps() + self.B(self.v_d).rdeps()
            for tt in range(NKT):
                vtok = p.dma("qL", v_sb[:, tt, :], self.v_d.ap()[tt][:, hg * 260:(hg + 1) * 260], deps=vdeps)
                self.B(self.v_d).read(vtok)
            self.B(v_sb).wrote(vtok)
            kdeps = self.B(k_sb).wdeps() + self.B(self.k_d).rdeps() + [ones_ms]
            for hh in range(4):
                h = hg * 4 + hh
                for qq in range(S // 512):
                    ktok = p.dma("qL", k_sb[0:64, hh * S + qq * 512:hh * S + (qq + 1) * 512],
                                 self.k_d.ap()[h, qq], deps=kdeps)
                    self.B(self.k_d).read(ktok)
            self.B(k_sb).wrote(ktok)
            for hh in range(4):
                kstok = p.op("dve", lambda e, hh=hh: e.tensor_copy(
                    ksub[:, hh * NS:(hh + 1) * NS],
                    k_sb[0:64, hh * S:(hh + 1) * S].rearrange("p (n s) -> p n s", s=CFG.SAMPLE)
                        [:, :, 0:1].rearrange("p n o -> p (n o)")),
                    deps=[ktok] + (self.B(ksub).wdeps() if hh == 0 else []))
            self.B(ksub).wrote(kstok)
            for qb in range(NQB):
                qdeps = self.B(q_sb).wdeps() + self.B(self.q_d).rdeps()
                for hh in range(4):
                    h = hg * 4 + hh
                    for j in range(QB // 512):
                        qtok = p.dma("qL", q_sb[0:64, hh * QB + j * 512:hh * QB + (j + 1) * 512],
                                     self.q_d.ap()[h, qb * (QB // 512) + j], deps=qdeps)
                        self.B(self.q_d).read(qtok)
                self.B(q_sb).wrote(qtok)
                for hh in range(4):
                    h = hg * 4 + hh
                    nqt = QB // 128
                    for ii in range(nqt):
                        mm = p.op("pe", lambda e, ii=ii, hh=hh: e.matmul(
                            self.ps_m[:, 0:NS],
                            q_sb[0:64, hh * QB + ii * 128:hh * QB + ii * 128 + 128],
                            ksub[:, hh * NS:(hh + 1) * NS], start=True, stop=True),
                            deps=[qtok, kstok] + self.pwd(self.ps_m, (0,)), inc=True)
                        self.pwrote0(self.ps_m, mm)
                        rd = p.op("dve", lambda e, ii=ii: e.reduce_max(
                            mneg[:, ii:ii + 1], self.ps_m[:, 0:NS], axis=AX.X, negate=True),
                            deps=[mm] + (self.B(mneg).wdeps() if ii == 0 else []))
                        self.pread0(self.ps_m, rd)
                    self.B(mneg).wrote(rd)
                    tr = p.op("pe", lambda e: e.transpose(self.ps_a[0:nqt, 0:128], mneg[:, 0:nqt],
                                                          self.c_id32[:, :]),
                              deps=[rd, self.B(self.c_id32).ready1] + self.pwd(self.ps_a), inc=True)
                    self.pwrote(self.ps_a, tr)
                    self.B(mneg).read(tr)
                    cv = p.op("dve", lambda e: e.tensor_copy(mnegb[:, :], self.ps_a[0:nqt, 0:128]),
                              deps=[tr] + self.B(mnegb).wdeps())
                    self.pread(self.ps_a, cv)
                    self.B(mnegb).wrote(cv)
                    dm = p.dma("qG", q_sb[64:65, hh * QB:(hh + 1) * QB].rearrange("p (a b) -> p a b", b=128),
                               mnegb[:, :], deps=[cv])
                    self.B(mnegb).read(dm)
                    self.B(q_sb).wrote(dm)
                    qtok2 = dm
                    simbufs = [self.ps_a, self.ps_b]
                    avdeps = self.pwd(self.ps_c)
                    av_last = None
                    exs = {}
                    for kt in range(NKT + 1):
                        if kt < NKT:
                            sel = kt % 2
                            ps = simbufs[sel]
                            sdeps = [qtok2, ktok, vtok] + self.pwd(ps)
                            for j in range(2):
                                mm = p.op("pe", lambda e, j=j, kt=kt, hh=hh, ps=ps: e.matmul(
                                    ps[:, j * 512:(j + 1) * 512],
                                    k_sb[0:65, hh * S + kt * 128:hh * S + kt * 128 + 128],
                                    q_sb[0:65, hh * QB + j * 512:hh * QB + (j + 1) * 512],
                                    start=True, stop=True),
                                    deps=(sdeps if j == 0 else []), inc=(j == 1))
                            self.pwrote(ps, mm)
                            ex = p.op("act", lambda e, sel=sel, ps=ps: e.activation(
                                pt[sel][:, :], ps[:, :], AF.Exp),
                                deps=[mm] + self.B(pt[sel]).wdeps())
                            self.pread(ps, ex)
                            self.B(pt[sel]).wrote(ex)
                            exs[kt] = ex
                        if kt > 0:
                            ka = kt - 1
                            sela = ka % 2
                            for j in range(2):
                                av = p.op("pe", lambda e, j=j, ka=ka, hh=hh, sela=sela: e.matmul(
                                    self.ps_c[0:65, j * 512:(j + 1) * 512],
                                    v_sb[:, ka, hh * 65:(hh + 1) * 65],
                                    pt[sela][:, j * 512:(j + 1) * 512],
                                    start=(ka == 0), stop=(ka == NKT - 1)),
                                    deps=([exs[ka]] + (avdeps if ka == 0 and j == 0 else [])),
                                    inc=(j == 1))
                            self.B(pt[sela]).read(av)
                            av_last = av
                    self.pwrote(self.ps_c, av_last)
                    self.B(q_sb).read(av_last)
                    self.B(k_sb).read(av_last)
                    self.B(v_sb).read(av_last)
                    self.B(ksub).read(av_last)
                    oev = p.op("dve", lambda e: e.tensor_copy(ostage[:, :], self.ps_c[0:65, 0:QB]),
                               deps=[av_last] + self.B(ostage).wdeps())
                    self.pread(self.ps_c, oev)
                    self.B(ostage).wrote(oev)
                    rc = p.op("act", lambda e: e.activation(rrow32[0:1, :], ostage[64:65, :], AF.Ln,
                                                            scale=2.0 ** -64),
                              deps=[oev] + self.B(rrow32).wdeps())
                    self.B(ostage).read(rc)
                    self.B(rrow32).wrote(rc)
                    rc = p.op("act", lambda e: e.activation(rrow[0:1, :], rrow32[0:1, :], AF.Exp,
                                                            scale=-1.0),
                              deps=[rc] + self.B(rrow).wdeps())
                    self.B(rrow32).read(rc)
                    self.B(rrow).wrote(rc)
                    for j in range(2):
                        bc = p.op("pe", lambda e, j=j: e.matmul(
                            self.ps_m[0:64, 0:512], self.c_ones1x[0:1, 0:64],
                            rrow[0:1, j * 512:(j + 1) * 512], start=True, stop=True),
                            deps=[rc] + self.pwd(self.ps_m, (0,)), inc=True)
                        self.pwrote0(self.ps_m, bc)
                        self.B(rrow).read(bc)
                        ec = p.op("dve", lambda e, j=j: e.tensor_copy(bcb[:, j * 512:(j + 1) * 512],
                                                                      self.ps_m[0:64, 0:512]),
                                  deps=[bc] + (self.B(bcb).wdeps() if j == 0 else []))
                        self.pread0(self.ps_m, ec)
                    self.B(bcb).wrote(ec)
                    row0 = 64 * (h % 2)
                    stx = p.op("dve", lambda e, row0=row0: e.scalar_tensor_tensor(
                        o2[row0:row0 + 64, :], ostage[0:64, :], 2.0 ** -64, bcb[:, :],
                        ALU.mult, ALU.mult),
                        deps=[ec, oev] + (self.B(o2).wdeps() if h % 2 == 0 else []))
                    self.B(ostage).read(stx)
                    self.B(bcb).read(stx)
                    self.B(o2).wrote(stx)
                    if h % 2 == 1:
                        cc = h // 2
                        for j in range(QB // 512):
                            d = p.dma("qS", self.o2_d.ap()[cc, qb * (QB // 512) + j],
                                      o2[:, j * 512:(j + 1) * 512], deps=[stx])
                            self.B(o2).read(d)
                            self.B(self.o2_d).wrote_add(d)

    def ph_outer_wos(self):
        p, S = self.p, self.S
        osb = self.sb("ow_o", [128, 8 * 512], BF16)
        wsb = self.sb("ow_w", [128, 16 * 512], BF16)
        aob = self.sb("ow_ao", [128, 8 * 512], BF16)
        ztm = self.sb("ow_z", [128, 1024], F32)
        z2s = self.sb("ow_z2", [128, 1024], F32)
        sout = self.sb("ow_s", [128, 1024], F32)
        stt = self.sb("ow_st", [128, 8], F32)
        xnt = self.sb("ow_xnt", [128, 1024], F32)
        cgam = self.sb("ow_gam", [128, CFG.D], F32)
        cbet = self.sb("ow_bet", [128, CFG.D], F32)
        self.B(cgam).wrote(p.dma("qL", cgam[:, :], self.gamma_in.ap(), deps=[]))
        self.B(cbet).wrote(p.dma("qL", cbet[:, :], self.beta_in.ap(), deps=[]))
        wdeps = self.B(wsb).wdeps() + self.B(self.woo_bf).rdeps()
        for fc in range(8):
            for g in range(2):
                wtok = p.dma("qV", wsb[:, (g * 8 + fc) * 512:(g * 8 + fc + 1) * 512],
                             self.woo_bf.ap()[fc, g], deps=wdeps)
                self.B(self.woo_bf).read(wtok)
        self.B(wsb).wrote(wtok)
        for qb in range(S // 512):
            odeps = self.B(osb).wdeps() + self.B(self.o2_d).rdeps()
            for cc in range(8):
                otok = p.dma("qL", osb[:, cc * 512:(cc + 1) * 512], self.o2_d.ap()[cc, qb], deps=odeps)
                self.B(self.o2_d).read(otok)
            self.B(osb).wrote(otok)
            for co in range(8):
                g, jj = co // 4, co % 4
                reg = co % 2
                pd = self.pwd(self.ps_a, (reg,))
                for fc in range(8):
                    mm = p.op("pe", lambda e, fc=fc, g=g, jj=jj, reg=reg: e.matmul(
                        self.ps_a[:, reg * 512:(reg + 1) * 512],
                        wsb[:, (g * 8 + fc) * 512 + jj * 128:(g * 8 + fc) * 512 + jj * 128 + 128],
                        osb[:, fc * 512:(fc + 1) * 512], start=(fc == 0), stop=(fc == 7)),
                        deps=[wtok, otok] + (pd if fc == 0 else []), inc=(fc == 7))
                self.pwrote(self.ps_a, mm, (reg,))
                self.B(wsb).read(mm)
                self.B(osb).read(mm)
                cv = p.op("dve", lambda e, co=co, reg=reg: e.tensor_copy(
                    aob[:, co * 512:(co + 1) * 512], self.ps_a[:, reg * 512:(reg + 1) * 512]),
                    deps=[mm] + (self.B(aob).wdeps() if co == 0 else []))
                self.pread(self.ps_a, cv, (reg,))
            self.B(aob).wrote(cv)
            aotok = cv
            for tl in range(4):
                tt = qb * 4 + tl
                pd = self.pwd(self.ps_t, (0,))
                for co in range(8):
                    tk = p.op("pe", lambda e, co=co, tl=tl: e.transpose(
                        self.ps_t[:, co * 128:(co + 1) * 128],
                        aob[:, co * 512 + tl * 128:co * 512 + tl * 128 + 128], self.c_idbf[:, :]),
                        deps=[aotok] + (pd if co == 0 else []), inc=(co == 7))
                self.pwrote0(self.ps_t, tk)
                self.B(aob).read(tk)
                lx = p.dma("qL", xnt[:, :], self.xn_tm.ap()[tt],
                           deps=self.B(xnt).wdeps() + self.B(self.xn_tm).rdeps())
                self.B(xnt).wrote(lx)
                self.B(self.xn_tm).read(lx)
                zt = p.op("dve", lambda e: e.scalar_tensor_tensor(ztm[:, :], self.ps_t[:, :], 1.0,
                                                                  xnt[:, :], ALU.bypass, ALU.add),
                          deps=[tk, lx] + self.B(ztm).wdeps())
                self.pread0(self.ps_t, zt)
                self.B(xnt).read(zt)
                self.B(ztm).wrote(zt)
                st = self._ln_rows(stt, z2s, ztm, zt)
                zn = p.op("dve", lambda e: e.tensor_scalar(sout[:, :], ztm[:, :], stt[:, 6:7], stt[:, 7:8],
                                                           ALU.mult, ALU.add),
                          deps=[st, zt] + self.B(sout).wdeps())
                self.B(stt).read(zn)
                self.B(ztm).read(zn)
                g1 = p.op("dve", lambda e: e.tensor_mul(sout[:, :], sout[:, :], cgam[:, :]),
                          deps=[zn, self.B(cgam).ready1])
                g2 = p.op("dve", lambda e: e.tensor_add(sout[:, :], sout[:, :], cbet[:, :]),
                          deps=[g1, self.B(cbet).ready1])
                self.B(sout).wrote(g2)
                d = p.dma("qS", self.s_tm.ap()[tt], sout[:, :], deps=[g2])
                self.B(sout).read(d)
                self.B(self.s_tm).wrote_add(d)

    def _shift_cast(self, ysrc, blk, xi_bf, ytile):
        p = self.p
        TB = CFG.TB
        t0 = blk * TB
        toks = []
        for cc in range(8):
            yb = self.B(ytile[cc % 2])
            ydeps = yb.wdeps()
            if blk > 0:
                ld = p.dma("qG", ytile[cc % 2][:, 0:TB + 1], ysrc.ap()[cc, :, t0 - 1:t0 + TB], deps=ydeps)
                self.B(ysrc).read(ld)
            else:
                ld = p.dma("qG", ytile[cc % 2][:, 1:TB + 1], ysrc.ap()[cc, :, 0:TB], deps=ydeps)
                self.B(ysrc).read(ld)
                ld = p.op("pool", lambda e, cc=cc: e.memset(ytile[cc % 2][:, 0:1], 0.0), deps=[ld])
            yb.wrote(ld)
            src = ytile[cc % 2][:, 1:TB + 1] if cc < 4 else ytile[cc % 2][:, 0:TB]
            ct = p.op("pool", lambda e, cc=cc, src=src: e.tensor_copy(xi_bf[:, cc * TB:(cc + 1) * TB], src),
                      deps=[ld] + (self.B(xi_bf).wdeps() if cc < 2 else []))
            yb.read(ct)
            toks.append(ct)
        self.B(xi_bf).wrote(toks[-1])
        return toks

    def _shift_dma1(self, ysrc, blk, cc, ytile):
        p = self.p
        TB = CFG.TB
        t0 = blk * TB
        yb = self.B(ytile[cc % 2])
        ld = p.dma("qG", ytile[cc % 2][:, 0:TB + 1], ysrc.ap()[cc, :, t0 - 1:t0 + TB],
                   deps=yb.wdeps())
        self.B(ysrc).read(ld)
        yb.wrote(ld)
        return ld

    def _shift_copy(self, cc, ld, xi_bf, ytile):
        p = self.p
        TB = CFG.TB
        yb = self.B(ytile[cc % 2])
        src = ytile[cc % 2][:, 1:TB + 1] if cc < 4 else ytile[cc % 2][:, 0:TB]
        ct = p.op("pool", lambda e, cc=cc, src=src: e.tensor_copy(
            xi_bf[:, cc * TB:(cc + 1) * TB], src),
            deps=[ld] + (self.B(xi_bf).wdeps() if cc < 2 else []))
        yb.read(ct)
        return ct

    def ph_inner_attn(self, l, ysrc, ydst, trickle=()):
        d = None
        p, S = self.p, self.S
        TB = CFG.TB
        NBLK = S // TB
        KH = TB + 256
        NSL = TB // 128 + 2
        sfx = f"_{l}"
        qbf = self.sb("ia_qbf" + sfx, [128, 8 * TB], BF16)
        qpre = self.sb("ia_qpre" + sfx, [128, 8 * TB], BF16)
        kfin = self.sb("ia_kfin" + sfx, [128, 8 * KH], BF16)
        vaug = self.sb("ia_vaug" + sfx, [128, NSL, 1040], BF16)
        rn = self.sb("ia_rn" + sfx, [16, TB], BF16)
        t1 = self.sb("ia_t1" + sfx, [128, TB], F32)
        t2 = self.sb("ia_t2" + sfx, [128, TB], F32)
        t4 = self.sb("ia_t4" + sfx, [16, 512], F32)
        pt = [self.sb("ia_pt0" + sfx, [128, TB], BF16), self.sb("ia_pt1" + sfx, [128, TB], BF16)]
        q2 = pt[0]
        sst = pt[1]
        tabs = self.sb("ia_tabs" + sfx, [128, 4 * TB], BF16)
        cst = self.sb("ia_cst" + sfx, [128, TB], BF16)
        wsbA = self.sb("ia_wA" + sfx, [128, 16 * 512], BF16)
        wsbB = self.sb("ia_wB" + sfx, [128, 16 * 512], BF16)
        rr = self.sb("ia_rr" + sfx, [1, TB], BF16)
        bcbu = self.sb("ia_bcb" + sfx, [64, TB], BF16)
        o2b = qbf
        xi, yt = self.ia_xi, self.ia_yt

        def load_w(wsb, pairs, srcd, l):
            deps = self.B(wsb).wdeps() + self.LB(srcd, l).rdeps()
            for (slot, g) in pairs:
                for fc in range(8):
                    wt = p.dma("qV", wsb[:, (slot * 8 + fc) * 512:(slot * 8 + fc + 1) * 512],
                               srcd.ap()[l, fc, g], deps=deps)
                    self.LB(srcd, l).read(wt)
            self.B(wsb).wrote(wt)
            return wt

        for blk in range(NBLK):
            t0 = blk * TB
            if blk == 0:
                hk = [p.op("pool", lambda e, cc=cc: e.memset(kfin[:, cc * KH:cc * KH + 256], 0.0),
                           deps=self.B(kfin).wdeps()) for cc in range(8)]
                hv = p.op("pool", lambda e: e.memset(vaug[:, 0:2, :], 0.0), deps=self.B(vaug).wdeps())
            else:
                hk = [p.op("pool", lambda e, cc=cc: e.tensor_copy(
                    kfin[:, cc * KH:cc * KH + 256], kfin[:, cc * KH + TB:cc * KH + TB + 256]),
                    deps=self.B(kfin).wdeps()) for cc in range(8)]
                hv = p.op("pool", lambda e: e.tensor_copy(vaug[:, 0:2, :], vaug[:, NSL - 2:NSL, :]),
                          deps=self.B(vaug).wdeps())
            tbuf = self.B(tabs)
            if blk == 0:
                xtoks = self._shift_cast(ysrc, blk, xi, yt)
                tdeps = tbuf.wdeps()
                for (i, dtab) in enumerate((self.d_tab_cs, self.d_tab_ss,
                                            self.d_tab_ci, self.d_tab_si)):
                    ttok = p.dma("qL", tabs[:, i * TB:(i + 1) * TB], dtab.ap()[:, t0:t0 + TB],
                                 deps=tdeps)
                tbuf.wrote(ttok)
            else:
                xtoks, ttok = next_xtoks, next_ttok
            xtok = xtoks[-1]
            for (is_q, svec, spvec, csoff, ssoff, slot_grps, wsb) in (
                    (True, self.qs8_sb, self.qs8p_sb, 0, TB, [(0, 0), (1, 1)], wsbA),
                    (False, self.ks_sb, self.ksp_sb, 2 * TB, 3 * TB, [(0, 2), (1, 3)], wsbB)):
                wtok = load_w(wsb, slot_grps, self.wqkv_bf, l)
                ssq_t = [self.ps_m, self.ps_c]
                ss_last = [None, None]
                for cc in range(8):
                    slot, jj = cc // 4, cc % 4
                    for j in range(2):
                        pd = self.pwd(self.ps_a, (j,))
                        for fc in range(8):
                            mm = p.op("pe", lambda e, fc=fc, slot=slot, jj=jj, j=j, wsb=wsb: e.matmul(
                                self.ps_a[:, j * 512:(j + 1) * 512],
                                wsb[:, (slot * 8 + fc) * 512 + jj * 128:(slot * 8 + fc) * 512 + jj * 128 + 128],
                                xi[:, fc * TB + j * 512:fc * TB + j * 512 + 512],
                                start=(fc == 0), stop=(fc == 7)),
                                deps=[wtok, xtok] + (pd if fc == 0 else []), inc=(fc == 7))
                        self.pwrote(self.ps_a, mm, (j,))
                        self.B(wsb).read(mm)
                        self.B(xi).read(mm)
                        cv = p.op("dve", lambda e, cc=cc, j=j: e.tensor_copy(
                            qbf[:, cc * TB + j * 512:cc * TB + j * 512 + 512],
                            self.ps_a[:, j * 512:(j + 1) * 512]),
                            deps=[mm] + (self.B(qbf).wdeps() if (cc == 0 and j == 0) else []))
                        self.pread(self.ps_a, cv, (j,))
                        sq = p.op("act", lambda e, cc=cc, j=j: e.activation(
                            q2[:, j * 512:(j + 1) * 512],
                            qbf[:, cc * TB + j * 512:cc * TB + j * 512 + 512], AF.Square),
                            deps=[cv] + self.B(q2).wdeps())
                        self.B(qbf).read(sq)
                        self.B(q2).wrote(sq)
                        ss = p.op("pe", lambda e, cc=cc, j=j: e.matmul(
                            ssq_t[j][0:16, 0:512], self.c_esel[:, cc * 16:(cc + 1) * 16],
                            q2[:, j * 512:(j + 1) * 512],
                            start=(cc == 0), stop=(cc == 7)),
                            deps=[sq, self.B(self.c_esel).ready1] +
                                 (self.pwd(ssq_t[j], (0,)) if cc == 0 else []), inc=True)
                        self.B(q2).read(ss)
                        ss_last[j] = ss
                self.B(qbf).wrote(cv)
                for j in range(2):
                    self.pwrote0(ssq_t[j], ss_last[j])
                    cl = p.op("dve", lambda e, j=j: e.tensor_scalar_max(
                        t4[0:16, 0:512], ssq_t[j][0:16, 0:512], 1e-24),
                        deps=[ss_last[j]] + self.B(t4).wdeps())
                    self.pread0(ssq_t[j], cl)
                    lnx = p.op("act", lambda e: e.activation(t4[0:16, 0:512], t4[0:16, 0:512], AF.Ln),
                               deps=[cl])
                    exx = p.op("act", lambda e, j=j: e.activation(rn[0:16, j * 512:(j + 1) * 512],
                                                                  t4[0:16, 0:512], AF.Exp, scale=-0.5),
                               deps=[lnx] + (self.B(rn).wdeps() if j == 0 else []))
                    self.B(t4).wrote(exx)
                    self.B(rn).wrote(exx)
                qbtok = self.B(qbf).ready1
                rntok = self.B(rn).ready1
                cq = p.op("dve", lambda e, svec=svec, csoff=csoff: e.tensor_scalar_mul(
                    cst[:, :], tabs[:, csoff:csoff + TB], svec[:, l:l + 1]),
                    deps=[ttok, self.B(svec).ready1] + self.B(cst).wdeps())
                tbuf.read(cq)
                self.B(cst).wrote(cq)
                sq2 = p.op("dve", lambda e, spvec=spvec, ssoff=ssoff: e.tensor_scalar_mul(
                    sst[:, :], tabs[:, ssoff:ssoff + TB], spvec[:, l:l + 1]),
                    deps=[ttok, self.B(spvec).ready1] + self.B(sst).wdeps())
                tbuf.read(sq2)
                self.B(sst).wrote(sq2)
                for cc in range(8):
                    pd = self.pwd(self.ps_b)
                    for j in range(2):
                        rq = p.op("pe", lambda e, cc=cc, j=j: e.matmul(
                            self.ps_b[:, j * 512:(j + 1) * 512], self.c_rperm[:, :],
                            qbf[:, cc * TB + j * 512:cc * TB + j * 512 + 512], start=True, stop=True),
                            deps=[qbtok, self.B(self.c_rperm).ready1] + (pd if j == 0 else []), inc=(j == 1))
                    self.pwrote(self.ps_b, rq)
                    self.B(qbf).read(rq)
                    pdm = self.pwd(self.ps_c)
                    for j in range(2):
                        rb = p.op("pe", lambda e, cc=cc, j=j: e.matmul(
                            self.ps_c[:, j * 512:(j + 1) * 512],
                            self.c_eprime[0:16, cc * 128:(cc + 1) * 128],
                            rn[0:16, j * 512:(j + 1) * 512], start=True, stop=True),
                            deps=[rntok, self.B(self.c_eprime).ready1] + (pdm if j == 0 else []), inc=(j == 1))
                    self.pwrote(self.ps_c, rb)
                    self.B(rn).read(rb)
                    p1 = p.op("pool", lambda e, cc=cc: e.tensor_mul(
                        t1[:, :], qbf[:, cc * TB:(cc + 1) * TB], cst[:, :]),
                        deps=[qbtok, cq] + self.B(t1).wdeps())
                    self.B(qbf).read(p1)
                    self.B(cst).read(p1)
                    self.B(t1).wrote(p1)
                    p2 = p.op("dve", lambda e: e.tensor_mul(
                        t2[:, :], self.ps_b[:, 0:TB], sst[:, :]),
                        deps=[rq, sq2] + self.B(t2).wdeps())
                    self.pread(self.ps_b, p2)
                    self.B(sst).read(p2)
                    self.B(t2).wrote(p2)
                    p3 = p.op("pool", lambda e: e.tensor_add(t1[:, :], t1[:, :], t2[:, :]),
                              deps=[p1, p2])
                    self.B(t2).read(p3)
                    self.B(t1).wrote(p3)
                    if is_q:
                        dsl = qpre[:, cc * TB:(cc + 1) * TB]
                        exdeps = self.B(qpre).wdeps() if cc == 0 else []
                    else:
                        dsl = kfin[:, cc * KH + 256:cc * KH + 256 + TB]
                        exdeps = [hk[cc]] + (self.B(kfin).wdeps() if cc == 0 else [])
                    p4 = p.op("dve", lambda e, dsl=dsl: e.tensor_mul(dsl, t1[:, :], self.ps_c[:, 0:TB]),
                              deps=[p3, rb] + exdeps)
                    self.B(t1).read(p4)
                    self.pread(self.ps_c, p4)
                if is_q:
                    self.B(qpre).wrote(p4)
                else:
                    self.B(kfin).wrote(p4)
            qtok = self.B(qpre).ready1
            ktok = self.B(kfin).ready1
            wsb = wsbA
            wtok = load_w(wsb, [(0, 4), (1, 5)], self.wqkv_bf, l)
            for tl in range(TB // 128):
                for g in range(2):
                    pd = self.pwd(self.ps_a, (g,))
                    for fc in range(8):
                        mm = p.op("pe", lambda e, fc=fc, g=g, tl=tl, wsb=wsb: e.matmul(
                            self.ps_a[:, g * 512:(g + 1) * 512],
                            xi[:, fc * TB + tl * 128:fc * TB + tl * 128 + 128],
                            wsb[:, (g * 8 + fc) * 512:(g * 8 + fc + 1) * 512],
                            start=(fc == 0), stop=(fc == 7)),
                            deps=[wtok, xtok] + (pd if fc == 0 else []), inc=(fc == 7))
                    self.pwrote(self.ps_a, mm, (g,))
                    self.B(wsb).read(mm)
                    self.B(xi).read(mm)
                    cv = p.op("dve", lambda e, tl=tl, g=g: e.tensor_copy(
                        vaug[:, tl + 2, :].rearrange("p (h j) -> p h j", j=65)[:, g * 8:(g + 1) * 8, 0:64],
                        self.ps_a[:, g * 512:(g + 1) * 512]),
                        deps=[mm, hv] + self.B(vaug).wdeps())
                    self.pread(self.ps_a, cv, (g,))
                    self.B(vaug).wrote(cv)
            on = p.op("pool", lambda e: e.memset(
                vaug[:, 2:NSL, :].rearrange("p t (h j) -> p t h j", j=65)[:, :, :, 64:65], 1.0),
                deps=[cv] + self.B(vaug).wdeps())
            self.B(vaug).wrote(on)
            vtok = on
            sh_lds = None
            if blk + 1 < NBLK:
                sh_lds = [self._shift_dma1(ysrc, blk + 1, cc, yt) for cc in range(2)]
                next_xtoks = []
                tdeps = tbuf.wdeps()
                for (i, dtab) in enumerate((self.d_tab_cs, self.d_tab_ss,
                                            self.d_tab_ci, self.d_tab_si)):
                    next_ttok = p.dma("qL", tabs[:, i * TB:(i + 1) * TB],
                                      dtab.ap()[:, t0 + TB:t0 + 2 * TB], deps=tdeps)
                tbuf.wrote(next_ttok)
            WPB = TB // CFG.WIN
            pend = [None]

            def emit_ln(pe):
                (h, evt, stg) = pe
                rc = p.op("act", lambda e, stg=stg: e.activation(stg[96:97, 0:TB], stg[64:65, 0:TB],
                                                                 AF.Ln, scale=2.0 ** -20),
                          deps=[evt])
                self.B(stg).read(rc)
                rc = p.op("act", lambda e, stg=stg: e.activation(rr[0:1, :], stg[96:97, 0:TB],
                                                                 AF.Exp, scale=-1.0),
                          deps=[rc] + self.B(rr).wdeps())
                self.B(stg).read(rc)
                self.B(rr).wrote(rc)
                return rc

            def emit_tail(pe, rc):
                (h, evt, stg) = pe
                cc, row = h // 2, 64 * (h % 2)
                for j in range(2):
                    bc = p.op("pe", lambda e, j=j: e.matmul(
                        self.ps_m[0:64, 0:512], self.c_ones1x[0:1, 0:64],
                        rr[0:1, j * 512:(j + 1) * 512], start=True, stop=True),
                        deps=[rc] + self.pwd(self.ps_m, (0,)), inc=True)
                    self.pwrote0(self.ps_m, bc)
                    self.B(rr).read(bc)
                    ecv = p.op("dve", lambda e, j=j: e.tensor_copy(
                        bcbu[:, j * 512:(j + 1) * 512], self.ps_m[0:64, 0:512]),
                        deps=[bc] + (self.B(bcbu).wdeps() if j == 0 else []))
                    self.pread0(self.ps_m, ecv)
                self.B(bcbu).wrote(ecv)
                stx = p.op("dve", lambda e, cc=cc, row=row, stg=stg: e.scalar_tensor_tensor(
                    o2b[row:row + 64, cc * TB:(cc + 1) * TB], stg[0:64, 0:TB], 2.0 ** -20,
                    bcbu[:, :], ALU.mult, ALU.mult),
                    deps=[ecv, evt] + (self.B(o2b).wdeps() if h == 0 else []))
                self.B(stg).read(stx)
                self.B(bcbu).read(stx)
                return stx

            for h in range(CFG.H):
                cc, row = h // 2, 64 * (h % 2)
                avdeps = self.pwd(self.ps_c)
                av_last = None
                mks = {}
                rc_prev = None
                for w in range(WPB + 1):
                    if w < WPB:
                        sel = w % 2
                        ps = [self.ps_a, self.ps_b][sel]
                        pd = self.pwd(ps)
                        for kt in range(4):
                            mm = p.op("pe", lambda e, kt=kt, w=w, cc=cc, row=row, ps=ps: e.matmul(
                                ps[:, kt * 256:(kt + 1) * 256],
                                kfin[row:row + 64, cc * KH + (2 * w + kt) * 128:cc * KH + (2 * w + kt) * 128 + 128],
                                qpre[row:row + 64, cc * TB + w * 256:cc * TB + w * 256 + 256],
                                start=True, stop=True),
                                deps=([qtok, ktok, vtok] + pd) if kt == 0 else [], inc=(kt == 3))
                        self.pwrote(ps, mm)
                        self.B(kfin).read(mm)
                        self.B(qpre).read(mm)
                        ex = p.op("act", lambda e, sel=sel, ps=ps: e.activation(
                            pt[sel][:, :], ps[:, :], AF.Exp, bias=self.negc_sb[:, l:l + 1]),
                            deps=[mm, self.B(self.negc_sb).ready1] + self.B(pt[sel]).wdeps())
                        self.pread(ps, ex)
                        mask = self.c_maskW0 if (blk == 0 and w == 0) else self.c_maskT
                        mk = p.op("dve", lambda e, sel=sel, mask=mask: e.tensor_mul(
                            pt[sel][:, :], pt[sel][:, :], mask[:, :]), deps=[ex, self.B(mask).ready1])
                        self.B(pt[sel]).wrote(mk)
                        mks[w] = mk
                    if w == 2 and pend[0] is not None:
                        rc_prev = emit_ln(pend[0])
                    if w > 0:
                        wa = w - 1
                        sela = wa % 2
                        for kt in range(4):
                            av = p.op("pe", lambda e, kt=kt, wa=wa, h=h, sela=sela: e.matmul(
                                self.ps_c[0:65, wa * 256:(wa + 1) * 256],
                                vaug[:, 2 * wa + kt, h * 65:(h + 1) * 65],
                                pt[sela][:, kt * 256:(kt + 1) * 256],
                                start=(kt == 0), stop=(kt == 3)),
                                deps=([mks[wa]] + (avdeps if (wa == 0 and kt == 0) else [])),
                                inc=(kt == 3))
                        self.B(pt[sela]).read(av)
                        self.B(vaug).read(av)
                        av_last = av
                if pend[0] is not None:
                    emit_tail(pend[0], rc_prev)
                self.pwrote(self.ps_c, av_last)
                stg = [t1, t2][h % 2]
                evt = p.op("dve", lambda e, stg=stg: e.tensor_copy(stg[0:65, 0:TB],
                                                                   self.ps_c[0:65, 0:TB]),
                           deps=[av_last] + self.B(stg).wdeps())
                self.pread(self.ps_c, evt)
                self.B(stg).wrote(evt)
                pend[0] = (h, evt, stg)
                if sh_lds is not None and h % 2 == 1:
                    cc2 = h // 2
                    next_xtoks.append(self._shift_copy(cc2, sh_lds[cc2], xi, yt))
                    if cc2 + 2 < 8:
                        sh_lds.append(self._shift_dma1(ysrc, blk + 1, cc2 + 2, yt))
                    if cc2 == 7:
                        self.B(xi).wrote(next_xtoks[-1])
            rc_prev = emit_ln(pend[0])
            stx = emit_tail(pend[0], rc_prev)
            self.B(o2b).wrote(stx)
            o2tok = stx
            wsb = wsbB
            wotok = load_w(wsb, [(0, 0), (1, 1)], self.wo_bf, l)
            for fo in range(8):
                slot, jj = fo // 4, fo % 4
                for j in range(2):
                    ysel = self.ia_yr[j % 2]
                    ly = p.dma("qG", ysel[:, :], ysrc.ap()[fo, :, t0 + j * 512:t0 + (j + 1) * 512],
                               deps=self.B(ysel).wdeps())
                    self.B(ysrc).read(ly)
                    self.B(ysel).wrote(ly)
                    pd = self.pwd(self.ps_b)
                    for fc in range(8):
                        mm = p.op("pe", lambda e, fc=fc, slot=slot, jj=jj, j=j, wsb=wsb: e.matmul(
                            self.ps_b[:, j * 512:(j + 1) * 512],
                            wsb[:, (slot * 8 + fc) * 512 + jj * 128:(slot * 8 + fc) * 512 + jj * 128 + 128],
                            o2b[:, fc * TB + j * 512:fc * TB + j * 512 + 512],
                            start=(fc == 0), stop=(fc == 7)),
                            deps=[wotok, o2tok] + (pd if fc == 0 else []), inc=(fc == 7))
                    self.pwrote(self.ps_b, mm)
                    self.B(wsb).read(mm)
                    self.B(o2b).read(mm)
                    rs = p.op("dve", lambda e, j=j, ysel=ysel: e.scalar_tensor_tensor(
                        self.ia_ym[:, :], self.ps_b[:, j * 512:(j + 1) * 512], 1.0, ysel[:, :],
                        ALU.bypass, ALU.add), deps=[mm, ly] + self.B(self.ia_ym).wdeps())
                    self.pread(self.ps_b, rs)
                    self.B(ysel).read(rs)
                    self.B(self.ia_ym).wrote(rs)
                    d = p.dma("qS", ydst.ap()[fo, :, t0 + j * 512:t0 + (j + 1) * 512], self.ia_ym[:, :],
                              deps=[rs])
                    self.B(self.ia_ym).read(d)
                    self.B(ydst).wrote_add(d)
            if trickle:
                T = len(trickle)
                for th in trickle[T * blk // NBLK:T * (blk + 1) // NBLK]:
                    th()

    def ph_inner_ffn_main(self, l, ysrc, ydst):
        p, S = self.p, self.S
        TB = CFG.TB
        NCA = CFG.NCA
        sfx = f"_{l}"
        xin = self.ia_xi
        rbb = self.sb("fm_rb" + sfx, [128, 512], BF16)
        mbb = self.sb("fm_mb" + sfx, [128, 512], BF16)
        gbuf = self.sb("fm_g" + sfx, [128, TB], BF16)
        z = self.sb("fm_z" + sfx, [128, NCA * TB], BF16)
        w1a = [self.sb(f"fm_wa{i}" + sfx, [128, 8 * 512], BF16) for i in range(2)]
        w1g = [self.sb(f"fm_wg{i}" + sfx, [128, 8 * 512], BF16) for i in range(2)]
        w2s = [self.sb(f"fm_w2{i}" + sfx, [128, 22 * 512], BF16) for i in range(2)]
        rows = self.fs_rows
        xi = self.ia_xi

        aseq, gseq = [], []
        for co in range(NCA - 1, -1, -1):
            if not aseq or aseq[-1] != co // 4:
                aseq.append(co // 4)
            if not gseq or gseq[-1] != (co + NCA) // 4:
                gseq.append((co + NCA) // 4)

        def load_w1(bufs, g, slot):
            buf = bufs[slot]
            wd = self.B(buf).wdeps() + self.LB(self.w1_bf, l).rdeps()
            for fc in range(8):
                wt = p.dma("qF", buf[:, fc * 512:(fc + 1) * 512], self.w1_bf.ap()[l, fc, g], deps=wd)
                self.LB(self.w1_bf, l).read(wt)
            self.B(buf).wrote(wt)
            return wt

        def load_w2(g, slot):
            buf = w2s[slot]
            wd = self.B(buf).wdeps() + self.LB(self.w2_bf, l).rdeps()
            for f2 in range(CFG.NC2):
                wt = p.dma("qF", buf[:, f2 * 512:(f2 + 1) * 512], self.w2_bf.ap()[l, f2, g], deps=wd)
                self.LB(self.w2_bf, l).read(wt)
            self.B(buf).wrote(wt)
            return wt

        astate = {}
        gstate = {}

        def ensure(bufs, state, seq, idx):
            for k in (idx, idx + 1):
                if k < len(seq):
                    slot = k % 2
                    if state.get(slot) != seq[k]:
                        state[slot] = seq[k]
                        state[("tok", slot)] = load_w1(bufs, seq[k], slot)
            return bufs[idx % 2], state[("tok", idx % 2)]

        x2, rows2, tmp = self.fs_x2, self.fs_rows, self.fs_tmp
        for blk in range(S // TB):
            t0 = blk * TB
            xtoks = self._shift_cast(ysrc, blk, xi, self.ia_yt)
            xtok = xtoks[-1]
            for j in range(TB // 512):
                pd = self.pwd(self.ps_m, (0,))
                pc = self.pwd(self.ps_c, (0,))
                for cc in range(8):
                    sq = p.op("act", lambda e, cc=cc, j=j: e.activation(
                        x2[:, :], xi[:, cc * TB + j * 512:cc * TB + j * 512 + 512], AF.Square),
                        deps=[xtoks[cc]] + self.B(x2).wdeps())
                    self.B(x2).wrote(sq)
                    self.B(xi).read(sq)
                    m1 = p.op("pe", lambda e, cc=cc, j=j: e.matmul(
                        self.ps_m[0:1, 0:512], self.c_ones128[:, 0:1],
                        xi[:, cc * TB + j * 512:cc * TB + j * 512 + 512],
                        start=(cc == 0), stop=(cc == 7)),
                        deps=[xtok, self.B(self.c_ones128).ready1] + (pd if cc == 0 else []), inc=(cc == 7))
                    self.B(xi).read(m1)
                    m2 = p.op("pe", lambda e, cc=cc: e.matmul(
                        self.ps_c[0:1, 0:512], self.c_ones128[:, 0:1], x2[:, :],
                        start=(cc == 0), stop=(cc == 7)),
                        deps=[sq] + (pc if cc == 0 else []), inc=True)
                    self.B(x2).read(m2)
                self.pwrote0(self.ps_m, m1)
                self.pwrote0(self.ps_c, m2)
                td = self.B(tmp).wdeps()
                A, Bs = tmp[0:1, 0:512], tmp[0:1, 512:1024]
                C, Dm = Bs, A
                mu = p.op("dve", lambda e, A=A: e.tensor_scalar_mul(A, self.ps_m[0:1, 0:512],
                                                                   1.0 / CFG.D), deps=[m1] + td)
                self.pread0(self.ps_m, mu)
                mu2 = p.op("act", lambda e, A=A, Bs=Bs: e.activation(Bs, A, AF.Square), deps=[mu])
                var = p.op("dve", lambda e, Bs=Bs: e.scalar_tensor_tensor(
                    Bs, self.ps_c[0:1, 0:512], 1.0 / CFG.D, Bs,
                    ALU.mult, ALU.subtract), deps=[mu2, m2])
                self.pread0(self.ps_c, var)
                ve = p.op("dve", lambda e, Bs=Bs: e.tensor_scalar_add(Bs, Bs, CFG.LN_EPS), deps=[var])
                ln = p.op("act", lambda e, Bs=Bs: e.activation(Bs, Bs, AF.Ln), deps=[ve])
                rst = p.op("act", lambda e, Bs=Bs, C=C: e.activation(C, Bs, AF.Exp, scale=-0.5), deps=[ln])
                mur = p.op("dve", lambda e, A=A, C=C, Dm=Dm: e.tensor_mul(Dm, A, C), deps=[rst])
                self.B(tmp).wrote(mur)
                off = t0 + j * 512
                c1 = p.op("dve", lambda e, off=off, C=C: e.tensor_copy(rows2[0:1, off:off + 512], C),
                          deps=[mur] + self.B(rows2).wdeps())
                c2 = p.op("dve", lambda e, off=off, Dm=Dm: e.tensor_copy(rows2[64:65, off:off + 512], Dm),
                          deps=[mur, c1])
                self.B(tmp).read(c2)
                self.B(rows2).wrote(c2)
            for j in range(TB // 512):
                off = t0 + j * 512
                rdep = self.B(rows).rdeps()
                b1 = p.op("pe", lambda e, off=off: e.matmul(
                    self.ps_m[:, 0:512], self.c_ones1x[0:1, :], rows[0:1, off:off + 512],
                    start=True, stop=True), deps=rdep + self.pwd(self.ps_m, (0,)), inc=True)
                self.pwrote0(self.ps_m, b1)
                self.B(rows).read(b1)
                e1 = p.op("dve", lambda e: e.tensor_copy(rbb[:, :], self.ps_m[:, 0:512]),
                          deps=[b1] + self.B(rbb).wdeps())
                self.pread0(self.ps_m, e1)
                self.B(rbb).wrote(e1)
                b2 = p.op("pe", lambda e, off=off: e.matmul(
                    self.ps_m[:, 0:512], self.c_ones65[64:65, :], rows[64:65, off:off + 512],
                    start=True, stop=True), deps=rdep + [e1, self.B(self.c_ones65).ready1], inc=True)
                self.pwrote0(self.ps_m, b2)
                self.B(rows).read(b2)
                e2 = p.op("dve", lambda e: e.tensor_copy(mbb[:, :], self.ps_m[:, 0:512]),
                          deps=[b2] + self.B(mbb).wdeps())
                self.pread0(self.ps_m, e2)
                self.B(mbb).wrote(e2)
                for cc in range(8):
                    u1 = p.op("pool", lambda e, cc=cc, j=j: e.tensor_mul(
                        xin[:, cc * TB + j * 512:cc * TB + j * 512 + 512],
                        xi[:, cc * TB + j * 512:cc * TB + j * 512 + 512], rbb[:, :]),
                        deps=[xtok, e1] + (self.B(xin).wdeps() if cc == 0 and j == 0 else []))
                    self.B(xi).read(u1)
                    self.B(rbb).read(u1)
                    u2 = p.op("pool", lambda e, cc=cc, j=j: e.tensor_sub(
                        xin[:, cc * TB + j * 512:cc * TB + j * 512 + 512],
                        xin[:, cc * TB + j * 512:cc * TB + j * 512 + 512], mbb[:, :]),
                        deps=[u1, e2])
                    self.B(mbb).read(u2)
                    u2 = p.op("dve", lambda e, cc=cc, j=j: e.tensor_scalar(
                        xin[:, cc * TB + j * 512:cc * TB + j * 512 + 512],
                        xin[:, cc * TB + j * 512:cc * TB + j * 512 + 512],
                        self.lng_sb[:, l * CFG.CH + cc:l * CFG.CH + cc + 1],
                        self.lnb_sb[:, l * CFG.CH + cc:l * CFG.CH + cc + 1],
                        ALU.mult, ALU.add),
                        deps=[u2, self.B(self.lng_sb).ready1, self.B(self.lnb_sb).ready1])
            self.B(xin).wrote(u2)
            xntok = u2
            ai = gi = 0
            for idx, co in enumerate(range(NCA - 1, -1, -1)):
                cog = co + NCA
                if idx > 0 and aseq[ai] != co // 4:
                    ai += 1
                if idx > 0 and gseq[gi] != cog // 4:
                    gi += 1
                bufA, wtA = ensure(w1a, astate, aseq, ai)
                bufG, wtB = ensure(w1g, gstate, gseq, gi)
                jjA, jjB = co % 4, cog % 4
                for j in range(2):
                    pd = self.pwd(self.ps_a, (j,))
                    for fc in range(8):
                        mm = p.op("pe", lambda e, fc=fc, jjB=jjB, j=j, bufG=bufG: e.matmul(
                            self.ps_a[:, j * 512:(j + 1) * 512],
                            bufG[:, fc * 512 + jjB * 128:fc * 512 + jjB * 128 + 128],
                            xin[:, fc * TB + j * 512:fc * TB + j * 512 + 512],
                            start=(fc == 0), stop=(fc == 7)),
                            deps=[wtB, xntok] + (pd if fc == 0 else []), inc=(fc == 7))
                    self.pwrote(self.ps_a, mm, (j,))
                    self.B(xin).read(mm)
                    self.B(bufG).read(mm)
                    vcolg = self.b1p_sb[:, l * 44 + cog:l * 44 + cog + 1]
                    gl = p.op("act", lambda e, j=j, vcolg=vcolg: e.activation(
                        gbuf[:, j * 512:(j + 1) * 512], self.ps_a[:, j * 512:(j + 1) * 512],
                        AF.Gelu, bias=vcolg),
                        deps=[mm, self.B(self.b1p_sb).ready1] + (self.B(gbuf).wdeps() if j == 0 else []))
                    self.pread(self.ps_a, gl, (j,))
                    self.B(gbuf).wrote(gl)
                    pd = self.pwd(self.ps_b, (j,))
                    for fc in range(8):
                        mm2 = p.op("pe", lambda e, fc=fc, jjA=jjA, j=j, bufA=bufA: e.matmul(
                            self.ps_b[:, j * 512:(j + 1) * 512],
                            bufA[:, fc * 512 + jjA * 128:fc * 512 + jjA * 128 + 128],
                            xin[:, fc * TB + j * 512:fc * TB + j * 512 + 512],
                            start=(fc == 0), stop=(fc == 7)),
                            deps=[wtA, xntok] + (pd if fc == 0 else []), inc=(fc == 7))
                    self.pwrote(self.ps_b, mm2, (j,))
                    self.B(xin).read(mm2)
                    self.B(bufA).read(mm2)
                    vcola = self.b1p_sb[:, l * 44 + co:l * 44 + co + 1]
                    zt = p.op("dve", lambda e, co=co, j=j, vcola=vcola: e.scalar_tensor_tensor(
                        z[:, co * TB + j * 512:co * TB + j * 512 + 512],
                        self.ps_b[:, j * 512:(j + 1) * 512], vcola,
                        gbuf[:, j * 512:(j + 1) * 512], ALU.add, ALU.mult),
                        deps=[mm2, gl, self.B(self.b1p_sb).ready1] +
                             (self.B(z).wdeps() if (co == NCA - 1 and j == 0) else []))
                    self.pread(self.ps_b, zt, (j,))
                    self.B(gbuf).read(zt)
                    self.B(z).wrote(zt)
            ztok = self.B(z).ready1
            w2state = getattr(self, "_w2state" + sfx, None)
            if w2state is None:
                w2state = {}
                setattr(self, "_w2state" + sfx, w2state)
            def ensure_w2(g):
                for k in (g, g + 1):
                    if k < 2:
                        slot = k % 2
                        if w2state.get(slot) != k:
                            w2state[slot] = k
                            w2state[("tok", slot)] = load_w2(k, slot)
                return w2s[g % 2], w2state[("tok", g % 2)]
            for g in range(2):
                bufW, wt2 = ensure_w2(g)
                for jj in range(4):
                    fo = g * 4 + jj
                    for j in range(2):
                        ysel = self.ia_yr[j % 2]
                        ly = p.dma("qG", ysel[:, :], ysrc.ap()[fo, :, t0 + j * 512:t0 + (j + 1) * 512],
                                   deps=self.B(ysel).wdeps())
                        self.B(ysrc).read(ly)
                        self.B(ysel).wrote(ly)
                        pd = self.pwd(self.ps_b, (j,))
                        for f2 in range(CFG.NC2):
                            mm = p.op("pe", lambda e, f2=f2, jj=jj, j=j, bufW=bufW: e.matmul(
                                self.ps_b[:, j * 512:(j + 1) * 512],
                                bufW[:, f2 * 512 + jj * 128:f2 * 512 + jj * 128 + 128],
                                z[:, f2 * TB + j * 512:f2 * TB + j * 512 + 512],
                                start=(f2 == 0), stop=(f2 == CFG.NC2 - 1)),
                                deps=[wt2, ztok] + (pd if f2 == 0 else []),
                                inc=(f2 == CFG.NC2 - 1))
                        self.B(bufW).read(mm)
                        self.pwrote(self.ps_b, mm, (j,))
                        self.B(z).read(mm)
                        b2col = self.b2_sb[:, l * CFG.CH + fo:l * CFG.CH + fo + 1]
                        rs = p.op("dve", lambda e, j=j, b2col=b2col, ysel=ysel: e.scalar_tensor_tensor(
                            self.ia_ym[:, :], self.ps_b[:, j * 512:(j + 1) * 512], b2col, ysel[:, :],
                            ALU.add, ALU.add),
                            deps=[mm, ly, self.B(self.b2_sb).ready1] + self.B(self.ia_ym).wdeps())
                        self.pread(self.ps_b, rs, (j,))
                        self.B(ysel).read(rs)
                        self.B(self.ia_ym).wrote(rs)
                        d = p.dma("qS", ydst.ap()[fo, :, t0 + j * 512:t0 + (j + 1) * 512],
                                  self.ia_ym[:, :], deps=[rs])
                        self.B(self.ia_ym).read(d)
                        self.B(ydst).wrote_add(d)

    def ph_final(self, ysrc):
        p, S = self.p, self.S
        yfb = self.sb("fin_y", [128, 1024], F32)
        gtm = self.sb("fin_g", [128, 1024], F32)
        stm = self.sb("fin_s", [128, 1024], F32)
        xtm = self.sb("fin_x", [128, 1024], F32)
        otm = self.sb("fin_o", [128, 1024], F32)
        for tt in range(S // 128):
            yd = self.B(yfb).wdeps()
            lys = []
            for cc in range(8):
                lys.append(p.dma("qL", yfb[:, cc * 128:(cc + 1) * 128],
                                 ysrc.ap()[cc, :, tt * 128:(tt + 1) * 128], deps=yd))
                self.B(ysrc).read(lys[-1])
            self.B(yfb).wrote(lys[-1])
            pd = self.pwd(self.ps_a)
            for k in range(8):
                tk = p.op("pe", lambda e, k=k: e.transpose(self.ps_a[:, k * 128:(k + 1) * 128],
                                                           yfb[:, k * 128:(k + 1) * 128], self.c_id32[:, :]),
                          deps=[lys[-1]] + (pd if k == 0 else []), inc=(k == 7))
            self.pwrote(self.ps_a, tk)
            self.B(yfb).read(tk)
            sl = p.op("act", lambda e: e.activation(gtm[:, :], self.ps_a[:, :], AF.Silu),
                      deps=[tk] + self.B(gtm).wdeps())
            self.pread(self.ps_a, sl)
            self.B(gtm).wrote(sl)
            ls = p.dma("qL", stm[:, :], self.s_tm.ap()[tt],
                       deps=self.B(stm).wdeps() + self.B(self.s_tm).rdeps())
            self.B(self.s_tm).read(ls)
            self.B(stm).wrote(ls)
            lxx = p.dma("qL", xtm[:, :], self.x_in.ap()[tt * 128:(tt + 1) * 128, :],
                        deps=self.B(xtm).wdeps())
            self.B(xtm).wrote(lxx)
            m1 = p.op("dve", lambda e: e.tensor_mul(otm[:, :], gtm[:, :], stm[:, :]),
                      deps=[sl, ls] + self.B(otm).wdeps())
            self.B(gtm).read(m1)
            self.B(stm).read(m1)
            m2 = p.op("dve", lambda e: e.tensor_add(otm[:, :], otm[:, :], xtm[:, :]), deps=[m1, lxx])
            self.B(xtm).read(m2)
            self.B(otm).wrote(m2)
            d = p.dma("qS", self.out_t.ap()[tt * 128:(tt + 1) * 128, :], otm[:, :], deps=[m2])
            self.B(otm).read(d)


def build_nc():
    nc = bass.Bass("TRN2", target_bir_lowering=False)
    from contextlib import ExitStack
    es = ExitStack()
    with es:
        p = Prog(nc)
        p.make_sems(es)
        k = Kern(nc, es, p)
        k.build()
        p.finalize_block(es)
    return nc


def make_inputs_common(ins):
    consts = host_constants(CFG.S)
    DI, F1H = CFG.DI, CFG.F1H
    b1 = np.asarray(ins["L0_ffn_b1"], dtype=np.float32)
    b1p = np.zeros((CFG.DEPTH, 2 * F1H), dtype=np.float32)
    b1p[:, 0:DI] = b1[:, 0:DI]
    b1p[:, F1H:F1H + DI] = b1[:, DI:2 * DI]
    f32 = lambda a: np.ascontiguousarray(np.asarray(a, dtype=np.float32))
    common = {
        "wq_in": f32(ins["L0_attn_Wq"]), "wk_in": f32(ins["L0_attn_Wk"]),
        "wv_in": f32(ins["L0_attn_Wv"]), "woo_in": f32(ins["L0_attn_Wo"]),
        "gamma_bc": f32(np.broadcast_to(np.asarray(ins["L0_attn_gamma"]), (128, CFG.D))),
        "beta_bc": f32(np.broadcast_to(np.asarray(ins["L0_attn_beta"]), (128, CFG.D))),
        "wqkv_in": f32(ins["L0_ffn_Wqkv"]), "qs_in": f32(ins["L0_ffn_q_scale"]),
        "ks_in": f32(ins["L0_ffn_k_scale"]), "wo_in": f32(ins["L0_ffn_Wo"]),
        "lng_in": f32(ins["L0_ffn_ln_g"]), "lnb_in": f32(ins["L0_ffn_ln_b"]),
        "w1_in": f32(ins["L0_ffn_W1"]), "b1p_in": b1p,
        "w2_in": f32(ins["L0_ffn_W2"]), "b2_in": f32(ins["L0_ffn_b2"]),
    }
    common.update(consts)
    return common


def kernel(**inputs):
    common = make_inputs_common(inputs)
    x = np.asarray(inputs["L0_attn_h"], dtype=np.float32)
    nc = build_nc()
    in_maps = []
    for b in range(8):
        m = dict(common)
        m["x_in"] = np.ascontiguousarray(x[b])
        in_maps.append(m)
    res = run_bass_kernel_spmd(nc, in_maps, core_ids=list(range(8)))
    return np.stack([r["out_t"] for r in res.results], axis=0)
